# version 15
# baseline (speedup 1.0000x reference)
import numpy as np
from contextlib import ExitStack

import concourse.bass as bass
import concourse.mybir as mybir
from concourse.bass_utils import run_bass_kernel_spmd

F32 = mybir.dt.float32
BF16 = mybir.dt.bfloat16
AF = mybir.ActivationFunctionType
ALU = mybir.AluOpType
AX = mybir.AxisListType

NCORES = 8
T = 1024
D = 2048
DC = 16
H = 16
HKV = 4
HD = 128
NE = 8
EI = 1408
EIT = 11
NSU = 12
NKC = 64
ISQ = float(1.0 / np.sqrt(128.0))
EPS = 1e-5

STAGE = 3

ENGS = ("sync", "gpsimd", "tensor", "vector", "scalar")


class Build:
    def __init__(self):
        self.ops = {e: [] for e in ENGS}
        self.cnt = {}

    def at(self, s):
        return self.cnt.get(s, 0)

    def emit(self, eng, fn, waits=(), incs=()):
        waits = [(s, v) for (s, v) in waits if v > 0]
        for s, a in incs:
            self.cnt[s] = self.cnt.get(s, 0) + a
        self.ops[eng].append((waits, fn, list(incs)))


class Slots:

    def __init__(self, n=2, prefix="s"):
        self.n = n
        self.prefix = prefix
        self.i = 0
        self.credit = {}
        self.uses = [0] * n

    def acquire(self):
        gi = self.i
        self.i += 1
        sl = gi % self.n
        wz = [("mm", self.credit[gi - self.n])] \
            if (gi - self.n) in self.credit else []
        self.uses[sl] += 1
        return gi, sl, wz, "%s%d" % (self.prefix, sl), 16 * self.uses[sl]

    def consumed(self, gi, cnt):
        self.credit[gi] = cnt


def build_program(stage):
    nc = bass.Bass("TRN2", target_bir_lowering=False)
    b = Build()

    ap = lambda name, shape, dt=F32, kind="ExternalInput": nc.dram_tensor(
        name, shape, dt, kind=kind).ap()
    d_x = ap("x", [D, T])
    d_cos = ap("cos_t", [HD, T])
    d_sin = ap("sin_t", [HD, T])
    d_wq = ap("wq", [D, D])
    d_wk = ap("wk", [D, HKV * HD])
    d_rollm = ap("rollm", [128, 128])
    d_wv = ap("wv", [D, HKV * HD])
    d_wo = ap("wo", [D, D])
    d_rwT = ap("rwT", [D, NE])
    d_sgw = ap("sgw", [128, DC])
    d_wg = ap("wg", [NE, D, EI]); d_wu = ap("wu", [NE, D, EI])
    d_wd = ap("wd", [NE, EI, D])
    d_swg = ap("swg", [D, 4 * EI]); d_swu = ap("swu", [D, 4 * EI])
    d_swd = ap("swd", [4 * EI, D])
    d_ones1 = ap("ones1", [128, 1])
    d_onesf = ap("onesf", [1, 128])
    d_halff = ap("halff", [1, 128])
    d_identb = ap("identb", [128, 128])
    d_sel = ap("sel05", [NE, NE * 128])
    d_sumsel = ap("sumsel", [128, 4])
    d_rowsel2 = ap("rowsel2", [2, 256])
    d_out = ap("out", [D, T], F32, "ExternalOutput")
    d_kin = nc.dram_tensor("kin", [HKV * HD, T], BF16, kind="Internal").ap()
    d_kout = nc.dram_tensor("kout", [NCORES * HKV * HD, T], BF16, kind="Internal").ap()
    d_vin = nc.dram_tensor("vin", [T, HKV * HD], BF16, kind="Internal").ap()
    d_vout = nc.dram_tensor("vout", [NCORES * T, HKV * HD], BF16, kind="Internal").ap()

    es = ExitStack()
    sb = lambda name, shape, dt, side="left": es.enter_context(
        nc.sbuf_tensor(name, shape, dt, side=side))

    x_sb = sb("x_sb", [128, DC * T], F32)
    ones1 = sb("ones1_s", [128, 1], BF16)
    onesf = sb("onesf_s", [1, 128], F32)
    halff = sb("halff_s", [1, 128], F32)
    identb = sb("identb_s", [128, 128], BF16)
    rollm_sb = sb("rollm_s", [128, 128], BF16)
    sel05 = sb("sel05_s", [NE, NE * 128], BF16)
    rwT_sb = sb("rwT_sb", [128, DC * NE], BF16)
    sgw_sb = sb("sgw_sb", [128, DC], BF16)
    sumsel = sb("sumsel_s", [128, 4], BF16)
    rowsel2 = sb("rowsel2_s", [2, 256], F32)
    rows = sb("rows", [1, 2 * 1024], F32)
    wgb = sb("wgb", [128, 2 * DC * 128], BF16, "right")
    wub = sb("wub", [128, 2 * DC * 128], BF16, "right")
    sqb = sb("sqb", [128, 2 * 1024], BF16, "right")
    xbb = sb("xbb", [128, 2 * 1024], BF16, "right")
    tscr = sb("tscr", [128, 2 * 1024], F32, "right")
    tscr_eq = sb("tscr_eq", [128, NE], F32, "right")
    lg_sb = sb("lg_sb", [128, 8 * NE], F32, "right")
    comb_sb = sb("comb_sb", [128, 8 * NE], BF16, "right")
    top8 = sb("top8", [128, 8 * NE], F32, "right")
    td = sb("td", [128, 8], F32, "right")
    w1s = sb("w1s", [128, 8], F32, "right")
    w2s = sb("w2s", [128, 8], F32, "right")
    combT_sb = sb("combT_sb", [NE, 1024], BF16, "right")

    sems = {}
    for s in ("sy", "syc", "gwc", "kvk", "kvk2", "kvv", "kvv2", "cc", "st",
              "mm", "ac", "ve",
              "g0", "g1", "u0", "u1", "d0", "d1", "d2", "d3", "v0", "v1"):
        sems[s] = es.enter_context(nc.semaphore(s))

    sl_wg = Slots(2, "g")
    sl_wu = Slots(2, "u")
    sl_wd = Slots(4, "d")
    sl_wv = Slots(2, "v")
    _uid = [0]

    def _nm(base):
        _uid[0] += 1
        return "%s_%d" % (base, _uid[0])

    b.emit("sync", lambda e: e.dma_start(
        out=x_sb[:].rearrange("p (k t) -> p k t", k=DC),
        in_=d_x.rearrange("(k p) t -> p k t", p=128)), incs=[("sy", 16)])
    w_x = b.at("sy")
    for out_ap, in_ap in (
            (ones1[:], d_ones1[:]),
            (identb[:], d_identb[:]),
            (rollm_sb[:], d_rollm[:]),
            (sel05[:], d_sel[:]),
            (rwT_sb[:].rearrange("p (c e) -> p c e", c=DC),
             d_rwT.rearrange("(c p) e -> p c e", p=128)),
            (sgw_sb[:], d_sgw[:]),
            (sumsel[:], d_sumsel[:])):
        b.emit("gpsimd", (lambda e, o=out_ap, i=in_ap: e.dma_start(out=o, in_=i)),
               incs=[("gwc", 16)])
    w_consts_gw = None
    b.emit("sync", lambda e: e.dma_start(out=onesf[:], in_=d_onesf[:]),
           incs=[("syc", 16)])
    b.emit("sync", lambda e: e.dma_start(out=halff[:], in_=d_halff[:]),
           incs=[("syc", 16)])
    b.emit("sync", lambda e: e.dma_start(out=rowsel2[:], in_=d_rowsel2[:]),
           incs=[("syc", 16)])
    w_consts_sy = b.at("syc")

    def emit_ln(y_sb, cw=None):
        if cw is None:
            cw = lambda c: [("sy", w_x)]
        with nc.psum_tensor(_nm("p_sx"), [1, 1024], F32) as p_sx, \
             nc.psum_tensor(_nm("p_sq"), [1, 1024], F32) as p_sq:
            sq_slots = [sqb[:, 0:1024], sqb[:, 1024:2048], y_sb[:, 0:1024]]
            xb_slots = [xbb[:, 0:1024], xbb[:, 1024:2048], y_sb[:, 1024:2048]]
            pe_done = {}
            NSL = 2
            for c in range(DC):
                sl = c % NSL
                wpe = [("mm", pe_done[c - NSL])] if c >= NSL else []
                wx0s = ([("gwc", w_consts_gw)] if c == 0 else [])
                b.emit("scalar", (lambda e, c=c, sl=sl: e.activation(
                    out=sq_slots[sl], in_=x_sb[:, c * T:(c + 1) * T],
                    func=AF.Square)),
                    waits=wpe + wx0s + cw(c), incs=[("ac", 1)])
                w_sq = b.at("ac")
                b.emit("vector", (lambda e, c=c, sl=sl: e.tensor_copy(
                    out=xb_slots[sl], in_=x_sb[:, c * T:(c + 1) * T])),
                    waits=wpe + cw(c), incs=[("ve", 1)])
                w_xb = b.at("ve")

                def mm_sums(e, c=c, sl=sl):
                    ins = None
                    for src, p in ((xb_slots[sl], p_sx), (sq_slots[sl], p_sq)):
                        for hh in range(2):
                            ins = e.matmul(
                                p[:, hh * 512:(hh + 1) * 512], ones1[:],
                                src[:, hh * 512:(hh + 1) * 512],
                                start=(c == 0), stop=(c == DC - 1))
                    return ins
                b.emit("tensor", mm_sums, waits=[("ac", w_sq), ("ve", w_xb)],
                       incs=[("mm", 1)])
                pe_done[c] = b.at("mm")
            w_sums = b.at("mm")
            m_row = rows[0:1, 0:1024]
            aux = rows[0:1, 1024:2048]
            mr_row = m_row
            t_row = tscr[0:1, 0:1024]
            b.emit("vector", lambda e: e.tensor_scalar_mul(m_row, p_sx[:], 1.0 / D),
                   waits=[("mm", w_sums), ("ve", b.at("ve"))], incs=[("ve", 1)])
            b.emit("vector", lambda e: e.tensor_scalar_mul(aux, p_sq[:], 1.0 / D),
                   waits=[("ve", b.at("ve"))], incs=[("ve", 1)])
            b.emit("vector", lambda e: e.tensor_mul(t_row, m_row, m_row),
                   waits=[("ve", b.at("ve")), ("ac", b.at("ac"))], incs=[("ve", 1)])
            b.emit("vector", lambda e: e.tensor_sub(aux, aux, t_row),
                   waits=[("ve", b.at("ve"))], incs=[("ve", 1)])
            w_var = b.at("ve")
            b.emit("vector", lambda e: e.tensor_scalar_add(aux, aux, EPS),
                   waits=[("ve", w_var)], incs=[("ve", 1)])
            b.emit("scalar", lambda e: e.activation(out=aux, in_=aux, func=AF.Ln),
                   waits=[("ve", b.at("ve"))], incs=[("ac", 1)])
            b.emit("scalar", lambda e: e.activation(out=aux, in_=aux, func=AF.Exp,
                                                    scale=-0.5),
                   waits=[("ac", b.at("ac"))], incs=[("ac", 1)])
            w_r = b.at("ac")
            b.emit("vector", lambda e: e.tensor_mul(mr_row, m_row, aux),
                   waits=[("ac", w_r), ("ve", b.at("ve"))], incs=[("ve", 1)])
            w_mr = b.at("ve")
        with nc.psum_tensor(_nm("p_rb"), [128, 1024], F32) as p_rb, \
             nc.psum_tensor(_nm("p_mb"), [128, 1024], F32) as p_mb:
            def mm_bc(e):
                ins = None
                for hh in range(2):
                    ins = e.matmul(p_rb[:, hh * 512:(hh + 1) * 512], onesf[:],
                                   aux[0:1, hh * 512:(hh + 1) * 512],
                                   start=True, stop=True)
                    ins = e.matmul(p_mb[:, hh * 512:(hh + 1) * 512], onesf[:],
                                   mr_row[0:1, hh * 512:(hh + 1) * 512],
                                   start=True, stop=True)
                return ins
            b.emit("tensor", mm_bc, waits=[("ve", w_mr), ("syc", w_consts_sy)],
                   incs=[("mm", 1)])
            w_bc = b.at("mm")
            b.emit("scalar", lambda e: e.copy(out=tscr[:, 0:1024], in_=p_rb[:]),
                   waits=[("mm", w_bc)], incs=[("ac", 1)])
            b.emit("scalar", lambda e: e.copy(out=sqb[:, 0:1024], in_=p_mb[:]),
                   waits=[("ac", b.at("ac"))], incs=[("ac", 1)])
            w_cp = b.at("ac")
            y_done = []
            for c in range(DC):
                b.emit("vector", (lambda e, c=c: e.tensor_mul(
                    y_sb[:, c * T:(c + 1) * T], x_sb[:, c * T:(c + 1) * T],
                    tscr[:, 0:1024])),
                    waits=([("ac", w_cp)] if c == 0 else []), incs=[("ve", 1)])
                b.emit("vector", (lambda e, c=c: e.tensor_sub(
                    y_sb[:, c * T:(c + 1) * T], y_sb[:, c * T:(c + 1) * T],
                    sqb[:, 0:1024])),
                    waits=[("ve", b.at("ve"))], incs=[("ve", 1)])
                y_done.append(b.at("ve"))
        return b.at("ve"), y_done

    def emit_router(y_sb, w_y):
        with nc.psum_tensor(_nm("p_lg0"), [128, NE], F32) as p_lg0, \
             nc.psum_tensor(_nm("p_lg1"), [128, NE], F32) as p_lg1:
            for tc in range(8):
                p_lg = p_lg0 if tc % 2 == 0 else p_lg1
                wz = [("ac", b.at("ac"))] if tc >= 2 else []

                def mm_lg(e, tc=tc, p_lg=p_lg):
                    ins = None
                    for c in range(DC):
                        ins = e.matmul(
                            p_lg[:], y_sb[:, c * T + tc * 128: c * T + (tc + 1) * 128],
                            rwT_sb[:, c * NE:(c + 1) * NE],
                            start=(c == 0), stop=(c == DC - 1))
                    return ins
                b.emit("tensor", mm_lg, waits=[("ve", w_y)] + wz, incs=[("mm", 1)])
                b.emit("scalar", (lambda e, tc=tc, p_lg=p_lg: e.copy(
                    out=lg_sb[:, tc * NE:(tc + 1) * NE], in_=p_lg[:])),
                    waits=[("mm", b.at("mm"))], incs=[("ac", 1)])
        w_lg = b.at("ac")
        with nc.psum_tensor(_nm("p_sg"), [1, 1024], F32) as p_sg:
            def mm_sg(e):
                ins = None
                for hh in range(2):
                    for c in range(DC):
                        ins = e.matmul(
                            p_sg[:, hh * 512:(hh + 1) * 512], sgw_sb[:, c:c + 1],
                            y_sb[:, c * T + hh * 512: c * T + (hh + 1) * 512],
                            start=(c == 0), stop=(c == DC - 1))
                return ins
            b.emit("tensor", mm_sg, waits=[("ve", w_y), ("ac", w_lg)],
                   incs=[("mm", 1)])
            b.emit("scalar", lambda e: e.activation(out=rows[0:1, 1024:2048],
                                                    in_=p_sg[:], func=AF.Sigmoid),
                   waits=[("mm", b.at("mm"))], incs=[("ac", 1)])
        for tc in range(8):
            b.emit("vector", (lambda e, tc=tc: e.max(
                out=top8[:, tc * NE:(tc + 1) * NE],
                in_=lg_sb[:, tc * NE:(tc + 1) * NE])),
                waits=[("ac", w_lg)] + ([("ve", b.at("ve"))] if tc == 0 else []),
                incs=[("ve", 1)])
        w_max = b.at("ve")
        for tc in range(8):
            b.emit("vector", (lambda e, tc=tc: e.tensor_sub(
                td[:, tc:tc + 1], top8[:, tc * NE:tc * NE + 1],
                top8[:, tc * NE + 1:tc * NE + 2])),
                waits=[("ve", w_max)] if tc == 0 else [], incs=[("ve", 1)])
        w_td = b.at("ve")
        b.emit("scalar", lambda e: e.activation(out=w1s[:], in_=td[:],
                                                func=AF.Sigmoid),
               waits=[("ve", w_td), ("ac", b.at("ac"))], incs=[("ac", 1)])
        b.emit("scalar", lambda e: e.activation(out=w2s[:], in_=td[:],
                                                func=AF.Sigmoid, scale=-1.0),
               waits=[("ac", b.at("ac"))], incs=[("ac", 1)])
        w_sig = b.at("ac")
        for tc in range(8):
            b.emit("vector", (lambda e, tc=tc: e.tensor_scalar(
                comb_sb[:, tc * NE:(tc + 1) * NE], lg_sb[:, tc * NE:(tc + 1) * NE],
                top8[:, tc * NE:tc * NE + 1], None, ALU.is_equal)),
                waits=[("ve", b.at("ve"))], incs=[("ve", 1)])
            b.emit("vector", (lambda e, tc=tc: e.tensor_scalar(
                tscr_eq[:], lg_sb[:, tc * NE:(tc + 1) * NE],
                top8[:, tc * NE + 1:tc * NE + 2], None, ALU.is_equal)),
                waits=[("ve", b.at("ve"))], incs=[("ve", 1)])
            b.emit("vector", (lambda e, tc=tc: e.tensor_sub(
                tscr_eq[:], tscr_eq[:], comb_sb[:, tc * NE:(tc + 1) * NE])),
                waits=[("ve", b.at("ve"))], incs=[("ve", 1)])
            b.emit("vector", lambda e: e.tensor_scalar_max(tscr_eq[:], tscr_eq[:], 0.0),
                   waits=[("ve", b.at("ve"))], incs=[("ve", 1)])
            b.emit("vector", (lambda e, tc=tc: e.tensor_scalar(
                comb_sb[:, tc * NE:(tc + 1) * NE], comb_sb[:, tc * NE:(tc + 1) * NE],
                w1s[:, tc:tc + 1], None, ALU.mult)),
                waits=[("ac", w_sig), ("ve", b.at("ve"))], incs=[("ve", 1)])
            b.emit("vector", (lambda e, tc=tc: e.tensor_scalar(
                tscr_eq[:], tscr_eq[:], w2s[:, tc:tc + 1], None, ALU.mult)),
                waits=[("ve", b.at("ve"))], incs=[("ve", 1)])
            b.emit("vector", (lambda e, tc=tc: e.tensor_add(
                comb_sb[:, tc * NE:(tc + 1) * NE], comb_sb[:, tc * NE:(tc + 1) * NE],
                tscr_eq[:])),
                waits=[("ve", b.at("ve"))], incs=[("ve", 1)])
        w_comb = b.at("ve")
        with nc.psum_tensor(_nm("p_tr0"), [NE, 128], BF16) as p_tr0, \
             nc.psum_tensor(_nm("p_tr1"), [NE, 128], BF16) as p_tr1:
            for tc in range(8):
                p_tr = p_tr0 if tc % 2 == 0 else p_tr1
                wz = [("ac", b.at("ac"))] if tc >= 2 else []
                b.emit("tensor", (lambda e, tc=tc, p_tr=p_tr: e.transpose(
                    p_tr[:], comb_sb[:, tc * NE:(tc + 1) * NE], identb[:])),
                    waits=[("ve", w_comb), ("gwc", w_consts_gw)] + wz,
                    incs=[("mm", 1)])
                b.emit("scalar", (lambda e, tc=tc, p_tr=p_tr: e.copy(
                    out=combT_sb[:, tc * 128:(tc + 1) * 128], in_=p_tr[:])),
                    waits=[("mm", b.at("mm"))], incs=[("ac", 1)])
        return b.at("ac")

    def emit_ffn(y_sb, sG, tmp, cbc_sb, h_sb, wdb, w_router):
        floor_mm = b.at("mm")
        floor_ve = b.at("ve")
        nload = [0]
        with nc.psum_tensor(_nm("p_g"), [128, 1024], F32) as p_g, \
             nc.psum_tensor(_nm("p_u"), [128, 1024], F32) as p_u, \
             nc.psum_tensor(_nm("p_d0"), [128, 1024], F32) as p_d0, \
             nc.psum_tensor(_nm("p_d1"), [128, 1024], F32) as p_d1:
            dn_evac = {}
            last_cbc_use = 0
            last_down_mm = 0
            for su in range(NSU):
                is_exp = su < NE
                if is_exp:
                    def mk_cb(e, su=su):
                        ins = None
                        for hh in range(2):
                            ins = e.matmul(
                                p_u[:, hh * 512:(hh + 1) * 512],
                                sel05[:, su * 128:(su + 1) * 128],
                                combT_sb[0:NE, hh * 512:(hh + 1) * 512],
                                start=True, stop=True)
                        return ins
                else:
                    def mk_cb(e, su=su):
                        ins = None
                        for hh in range(2):
                            ins = e.matmul(
                                p_u[:, hh * 512:(hh + 1) * 512], halff[:],
                                rows[0:1, 1024 + hh * 512: 1024 + (hh + 1) * 512],
                                start=True, stop=True)
                        return ins
                b.emit("tensor", mk_cb,
                       waits=[("ac", w_router), ("ve", b.at("ve"))], incs=[("mm", 1)])
                b.emit("scalar", lambda e: e.copy(out=cbc_sb[:], in_=p_u[:]),
                       waits=[("mm", b.at("mm")), ("ve", last_cbc_use)],
                       incs=[("ac", 1)])
                w_cbc = b.at("ac")
                h_ready_after = last_down_mm
                h_done = {}
                for j in range(EIT):
                    gi_g, slg, wzg, smg, tgg = sl_wg.acquire()
                    gi_u, slu, wzu, smu, tgu = sl_wu.acquire()
                    if is_exp:
                        src_g = d_wg[su][:, j * 128:(j + 1) * 128]
                        src_u = d_wu[su][:, j * 128:(j + 1) * 128]
                    else:
                        g4 = su - NE
                        src_g = d_swg[:, g4 * EI + j * 128: g4 * EI + (j + 1) * 128]
                        src_u = d_swu[:, g4 * EI + j * 128: g4 * EI + (j + 1) * 128]
                    b.emit("gpsimd", (lambda e, sl=slg, s=src_g: e.dma_start(
                        out=wgb[:, sl * 2048:(sl + 1) * 2048].rearrange(
                            "p (c i) -> p c i", c=DC),
                        in_=s.rearrange("(c p) i -> p c i", p=128))),
                        waits=wzg, incs=[(smg, 16)])
                    assert b.at(smg) == tgg
                    b.emit("gpsimd", (lambda e, sl=slu, s=src_u: e.dma_start(
                        out=wub[:, sl * 2048:(sl + 1) * 2048].rearrange(
                            "p (c i) -> p c i", c=DC),
                        in_=s.rearrange("(c p) i -> p c i", p=128))),
                        waits=wzu, incs=[(smu, 16)])
                    assert b.at(smu) == tgu

                    def mm_g(e, sl=slg):
                        ins = None
                        for c in range(DC):
                            for hh in range(2):
                                ins = e.matmul(
                                    p_g[:, hh * 512:(hh + 1) * 512],
                                    wgb[:, sl * 2048 + c * 128: sl * 2048 + (c + 1) * 128],
                                    y_sb[:, c * T + hh * 512: c * T + (hh + 1) * 512],
                                    start=(c == 0), stop=(c == DC - 1))
                        return ins
                    b.emit("tensor", mm_g,
                           waits=[(smg, tgg), ("ac", b.at("ac"))], incs=[("mm", 1)])
                    sl_wg.consumed(gi_g, b.at("mm"))
                    w_gmm = b.at("mm")

                    def mm_u(e, sl=slu):
                        ins = None
                        for c in range(DC):
                            for hh in range(2):
                                ins = e.matmul(
                                    p_u[:, hh * 512:(hh + 1) * 512],
                                    wub[:, sl * 2048 + c * 128: sl * 2048 + (c + 1) * 128],
                                    y_sb[:, c * T + hh * 512: c * T + (hh + 1) * 512],
                                    start=(c == 0), stop=(c == DC - 1))
                        return ins
                    b.emit("tensor", mm_u,
                           waits=[(smu, tgu), ("ac", w_cbc), ("ve", b.at("ve"))],
                           incs=[("mm", 1)])
                    sl_wu.consumed(gi_u, b.at("mm"))
                    w_umm = b.at("mm")
                    b.emit("scalar", lambda e: e.activation(out=sG[:], in_=p_g[:],
                                                            func=AF.Silu),
                           waits=[("mm", w_gmm), ("ve", b.at("ve"))], incs=[("ac", 1)])
                    w_silu = b.at("ac")
                    b.emit("vector", lambda e: e.tensor_mul(tmp[:], sG[:], cbc_sb[:]),
                           waits=[("ac", w_silu)], incs=[("ve", 1)])
                    hw = [("ve", b.at("ve")), ("mm", w_umm)]
                    if j == 0:
                        hw.append(("mm", h_ready_after))
                    b.emit("vector", (lambda e, j=j: e.tensor_mul(
                        h_sb[:, j * T:(j + 1) * T], tmp[:], p_u[:])),
                        waits=hw, incs=[("ve", 1)])
                    last_cbc_use = b.at("ve")
                    h_done[j] = b.at("ve")
                w_h = b.at("ve")
                for dt in range(DC):
                    pd = p_d0 if dt % 2 == 0 else p_d1
                    gi_d, sld, wzd, smd, tgd = sl_wd.acquire()
                    if is_exp:
                        src_d = d_wd[su][:, dt * 128:(dt + 1) * 128]
                    else:
                        g4 = su - NE
                        src_d = d_swd[g4 * EI:(g4 + 1) * EI, dt * 128:(dt + 1) * 128]
                    if nload[0] < 4:
                        wzd = wzd + [("mm", floor_mm), ("ve", floor_ve)]
                        nload[0] += 1
                    b.emit("gpsimd", (lambda e, sl=sld, s=src_d: e.dma_start(
                        out=wdb[:, sl * EIT * 128:(sl + 1) * EIT * 128].rearrange(
                            "p (i d) -> p i d", i=EIT),
                        in_=s.rearrange("(i p) d -> p i d", p=128))),
                        waits=wzd, incs=[(smd, 16)])
                    assert b.at(smd) == tgd
                    wz2 = [("ve", dn_evac[dt - 2])] if dt >= 2 else \
                          [("ve", b.at("ve") if su == 0 else dn_evac.get(DC - 2 + (dt % 2), 0))]

                    def mm_d(e, sl=sld, pd=pd, irange=None):
                        ins = None
                        for hh in range(2):
                            for i in irange:
                                ins = e.matmul(
                                    pd[:, hh * 512:(hh + 1) * 512],
                                    wdb[:, sl * EIT * 128 + i * 128: sl * EIT * 128 + (i + 1) * 128],
                                    h_sb[:, i * T + hh * 512: i * T + (hh + 1) * 512],
                                    start=(i == 0), stop=(i == EIT - 1))
                        return ins
                    if dt < 2:
                        b.emit("tensor",
                               (lambda e, sl=sld, pd=pd: mm_d(e, sl, pd,
                                                              range(EIT - 1))),
                               waits=[(smd, tgd), ("ve", h_done[EIT - 2])] + wz2,
                               incs=[("mm", 1)])
                        b.emit("tensor",
                               (lambda e, sl=sld, pd=pd: mm_d(e, sl, pd,
                                                              [EIT - 1])),
                               waits=[("ve", w_h)], incs=[("mm", 1)])
                    else:
                        b.emit("tensor",
                               (lambda e, sl=sld, pd=pd: mm_d(e, sl, pd,
                                                              range(EIT))),
                               waits=[(smd, tgd), ("ve", w_h)] + wz2,
                               incs=[("mm", 1)])
                    sl_wd.consumed(gi_d, b.at("mm"))
                    last_down_mm = b.at("mm")
                    w_dmm = b.at("mm")
                    b.emit("vector", (lambda e, dt=dt, pd=pd: e.tensor_add(
                        x_sb[:, dt * T:(dt + 1) * T], x_sb[:, dt * T:(dt + 1) * T],
                        pd[:])),
                        waits=[("mm", w_dmm)], incs=[("ve", 1)])
                    dn_evac[dt] = b.at("ve")
        return dict(dn_evac)

    ctxA = ExitStack()
    cos_sb = ctxA.enter_context(nc.sbuf_tensor("cos_sb", [128, 2 * T], BF16))
    y_sb = ctxA.enter_context(nc.sbuf_tensor("y_sb", [128, DC * T], BF16))
    b.emit("gpsimd", lambda e: e.dma_start(out=cos_sb[:, 0:T], in_=d_cos[:]),
           incs=[("gwc", 16)])
    b.emit("gpsimd", lambda e: e.dma_start(out=cos_sb[:, T:2 * T], in_=d_sin[:]),
           incs=[("gwc", 16)])
    w_cos = b.at("gwc")
    w_consts_gw = w_cos

    with nc.sbuf_tensor("fscr", [128, 3 * 1024], F32, side="right") as fscr, \
         nc.sbuf_tensor("h_sb", [128, EIT * T], BF16, side="right") as h_sb, \
         nc.sbuf_tensor("wdb1", [128, 4 * EIT * 128], BF16, side="right") as wdb1:
        w_y, _ = emit_ln(y_sb)
        w_router = emit_router(y_sb, w_y)
        ffn1_evac = emit_ffn(y_sb, fscr[:, 0:1024], fscr[:, 1024:2048],
                             fscr[:, 2048:3072], h_sb, wdb1, w_router)

    w_qrot = 0
    if stage >= 2:
        w_y2, y2_done = emit_ln(y_sb, cw=lambda c: [("ve", ffn1_evac[c])])
        ctxQ = ExitStack()
        q_sb = ctxQ.enter_context(nc.sbuf_tensor("q_sb", [128, H * T], BF16,
                                                 side="right"))
        mq2 = ctxQ.enter_context(nc.sbuf_tensor("mq2", [1, 64], F32, side="right"))
        with nc.sbuf_tensor("krot", [128, HKV * T], BF16, side="right") as krot, \
             nc.sbuf_tensor("vloc", [128, 8 * 512], BF16, side="right") as vloc, \
             nc.sbuf_tensor("wvb", [128, 2 * 512], BF16, side="right") as wvb:
            with ExitStack() as vs:
                vps = [vs.enter_context(nc.psum_tensor(_nm("p_v%d" % i), [128, 512], F32))
                       for i in range(8)]
                w_vmm = 0
                for c in range(DC):
                    gi_v, sl, wz, smv, tgv = sl_wv.acquire()
                    if wz == [] and c >= 2:
                        wz = [("mm", b.at("mm"))]
                    b.emit("gpsimd", (lambda e, c=c, sl=sl: e.dma_start(
                        out=wvb[:, sl * 512:(sl + 1) * 512],
                        in_=d_wv[c * 128:(c + 1) * 128, :])),
                        waits=wz, incs=[(smv, 16)])

                    def mm_v(e, c=c, sl=sl):
                        ins = None
                        for tc in range(8):
                            ins = e.matmul(
                                vps[tc][:],
                                y_sb[:, c * T + tc * 128: c * T + (tc + 1) * 128],
                                wvb[:, sl * 512:(sl + 1) * 512],
                                start=(c == 0), stop=(c == DC - 1))
                        return ins
                    b.emit("tensor", mm_v,
                           waits=[(smv, tgv), ("ve", y2_done[c])],
                           incs=[("mm", 1)])
                    sl_wv.consumed(gi_v, b.at("mm"))
                    w_vmm = b.at("mm")
                for tc in range(8):
                    b.emit("scalar", (lambda e, tc=tc: e.copy(
                        out=vloc[:, tc * 512:(tc + 1) * 512], in_=vps[tc][:])),
                        waits=[("mm", w_vmm)], incs=[("ac", 1)])
            w_vloc = b.at("ac")
            with nc.psum_tensor(_nm("p_k"), [128, 2 * T], F32) as p_k, \
                 nc.psum_tensor(_nm("p_kr"), [128, 2 * T], F32) as p_kr:
                pend = [None]
                copy_ac = {}
                for g in range(HKV):
                    sl = g % 2
                    gi_g, slg, wzg, smg, tgg = sl_wg.acquire()
                    b.emit("gpsimd", (lambda e, g=g, sl=slg: e.dma_start(
                        out=wgb[:, sl * 2048:(sl + 1) * 2048].rearrange(
                            "p (c i) -> p c i", c=DC),
                        in_=d_wk[:, g * 128:(g + 1) * 128].rearrange(
                            "(c p) i -> p c i", p=128))),
                        waits=wzg, incs=[(smg, 16)])
                    wpsum = [("ac", copy_ac[g - 2])] if g >= 2 else [("ac", w_vloc)]

                    def mm_k(e, sl=sl, slg=slg):
                        ins = None
                        for c in range(DC):
                            for hh in range(2):
                                ins = e.matmul(
                                    p_k[:, sl * T + hh * 512: sl * T + (hh + 1) * 512],
                                    wgb[:, slg * 2048 + c * 128: slg * 2048 + (c + 1) * 128],
                                    y_sb[:, c * T + hh * 512: c * T + (hh + 1) * 512],
                                    start=(c == 0), stop=(c == DC - 1))
                        return ins
                    b.emit("tensor", mm_k,
                           waits=[(smg, tgg), ("ve", w_y2), ("gwc", w_cos)] + wpsum,
                           incs=[("mm", 1)])
                    sl_wg.consumed(gi_g, b.at("mm"))
                    w_kmm = b.at("mm")
                    if pend[0] is not None:
                        pend[0]()

                    def mk_tail(g=g, sl=sl, w_kmm=w_kmm):
                        def _():
                            b.emit("scalar", (lambda e: e.copy(
                                out=krot[:, g * T:(g + 1) * T],
                                in_=p_k[:, sl * T:(sl + 1) * T])),
                                waits=[("mm", w_kmm)], incs=[("ac", 1)])
                            w_kn = b.at("ac")
                            copy_ac[g] = w_kn

                            def mm_kroll(e):
                                ins = None
                                for hh in range(2):
                                    ins = e.matmul(
                                        p_kr[:, sl * T + hh * 512: sl * T + (hh + 1) * 512],
                                        rollm_sb[:],
                                        krot[:, g * T + hh * 512: g * T + (hh + 1) * 512],
                                        start=True, stop=True)
                                return ins
                            wroll = [("ac", w_kn)]
                            wroll += [("ve", b.at("ve"))] if g >= 2 else                                      [("ac", w_vloc)]
                            b.emit("tensor", mm_kroll, waits=wroll, incs=[("mm", 1)])
                            w_rollmm = b.at("mm")
                            b.emit("vector", (lambda e: e.tensor_mul(
                                tscr[:, 0:T], krot[:, g * T:(g + 1) * T],
                                cos_sb[:, 0:T])),
                                waits=[("ac", w_kn), ("ve", b.at("ve"))],
                                incs=[("ve", 1)])
                            b.emit("vector", (lambda e: e.tensor_mul(
                                tscr[:, T:2 * T], p_kr[:, sl * T:(sl + 1) * T],
                                cos_sb[:, T:2 * T])),
                                waits=[("mm", w_rollmm), ("ve", b.at("ve"))],
                                incs=[("ve", 1)])
                            b.emit("vector", (lambda e: e.tensor_add(
                                krot[:, g * T:(g + 1) * T], tscr[:, 0:T],
                                tscr[:, T:2 * T])),
                                waits=[("ve", b.at("ve"))], incs=[("ve", 1)])
                        return _
                    pend[0] = mk_tail()
                pend[0]()
            w_krot = b.at("ve")
            b.emit("sync", lambda e: e.dma_start(
                out=d_kin.rearrange("(g p) t -> p g t", p=128),
                in_=krot[:].rearrange("p (g t) -> p g t", g=HKV)),
                waits=[("ve", w_krot)], incs=[("st", 16)])
            b.emit("sync", lambda e: e.dma_start(
                out=d_vin.rearrange("(c p) h -> p c h", p=128),
                in_=vloc[:].rearrange("p (c h) -> p c h", c=8)),
                waits=[("ac", w_vloc)], incs=[("st", 16)])
            w_kvst = b.at("st")
            b.emit("gpsimd", lambda e: e.collective_compute(
                "AllGather", ALU.bypass, replica_groups=[list(range(NCORES))],
                ins=[d_kin.opt()], outs=[d_kout.opt()]),
                waits=[("st", w_kvst)], incs=[("cc", 1)])
            b.emit("gpsimd", lambda e: e.collective_compute(
                "AllGather", ALU.bypass, replica_groups=[list(range(NCORES))],
                ins=[d_vin.opt()], outs=[d_vout.opt()]),
                incs=[("cc", 1)])
            w_cc = b.at("cc")
            with nc.psum_tensor(_nm("p_q"), [128, 2 * T], F32) as p_q, \
                 nc.psum_tensor(_nm("p_qr"), [128, 2 * T], F32) as p_qr:
                pend = [None]
                copy_ac = {}
                for h in range(H):
                    sl = h % 2
                    gi_g, slg, wzg, smg, tgg = sl_wg.acquire()
                    b.emit("gpsimd", (lambda e, h=h, sl=slg: e.dma_start(
                        out=wgb[:, sl * 2048:(sl + 1) * 2048].rearrange(
                            "p (c i) -> p c i", c=DC),
                        in_=d_wq[:, h * 128:(h + 1) * 128].rearrange(
                            "(c p) i -> p c i", p=128))),
                        waits=wzg, incs=[(smg, 16)])
                    wpsum = [("ac", copy_ac[h - 2])] if h >= 2 else [("ve", w_krot)]

                    def mm_q(e, sl=sl, slg=slg):
                        ins = None
                        for c in range(DC):
                            for hh in range(2):
                                ins = e.matmul(
                                    p_q[:, sl * T + hh * 512: sl * T + (hh + 1) * 512],
                                    wgb[:, slg * 2048 + c * 128: slg * 2048 + (c + 1) * 128],
                                    y_sb[:, c * T + hh * 512: c * T + (hh + 1) * 512],
                                    start=(c == 0), stop=(c == DC - 1))
                        return ins
                    b.emit("tensor", mm_q, waits=[(smg, tgg)] + wpsum,
                           incs=[("mm", 1)])
                    sl_wg.consumed(gi_g, b.at("mm"))
                    w_qmm = b.at("mm")
                    if pend[0] is not None:
                        pend[0]()

                    def mk_tail(h=h, sl=sl, w_qmm=w_qmm):
                        def _():
                            b.emit("scalar", (lambda e: e.copy(
                                out=q_sb[:, h * T:(h + 1) * T],
                                in_=p_q[:, sl * T:(sl + 1) * T])),
                                waits=[("mm", w_qmm)], incs=[("ac", 1)])
                            w_qn = b.at("ac")
                            copy_ac[h] = w_qn

                            def mm_qroll(e):
                                ins = None
                                for hh in range(2):
                                    ins = e.matmul(
                                        p_qr[:, sl * T + hh * 512: sl * T + (hh + 1) * 512],
                                        rollm_sb[:],
                                        q_sb[:, h * T + hh * 512: h * T + (hh + 1) * 512],
                                        start=True, stop=True)
                                return ins
                            wroll = [("ac", w_qn)]
                            wroll += [("ve", b.at("ve"))] if h >= 2 else                                      [("ve", w_krot)]
                            b.emit("tensor", mm_qroll, waits=wroll, incs=[("mm", 1)])
                            w_rollmm = b.at("mm")
                            b.emit("vector", (lambda e: e.tensor_mul(
                                tscr[:, 0:T], q_sb[:, h * T:(h + 1) * T],
                                cos_sb[:, 0:T])),
                                waits=[("ac", w_qn), ("ve", b.at("ve"))],
                                incs=[("ve", 1)])
                            b.emit("vector", (lambda e: e.tensor_mul(
                                tscr[:, T:2 * T], p_qr[:, sl * T:(sl + 1) * T],
                                cos_sb[:, T:2 * T])),
                                waits=[("mm", w_rollmm), ("ve", b.at("ve"))],
                                incs=[("ve", 1)])
                            b.emit("vector", (lambda e: e.tensor_add(
                                q_sb[:, h * T:(h + 1) * T], tscr[:, 0:T],
                                tscr[:, T:2 * T])),
                                waits=[("ve", b.at("ve"))], incs=[("ve", 1)])
                        return _
                    pend[0] = mk_tail()
                pend[0]()
            w_qrot = b.at("ve")
            with nc.psum_tensor(_nm("p_nq"), [1, 1024], F32) as p_nq:
                for h in range(H):
                    sl = h % 2
                    wz = [("mm", b.at("mm"))] if h >= 2 else []
                    b.emit("scalar", (lambda e, h=h, sl=sl: e.activation(
                        out=sqb[:, sl * 1024:(sl + 1) * 1024],
                        in_=q_sb[:, h * T:(h + 1) * T], func=AF.Square)),
                        waits=[("ve", w_qrot)] + wz, incs=[("ac", 1)])
                    w_qs = b.at("ac")
                    wz2 = [("ve", b.at("ve"))] if h >= 1 else []

                    def mm_nq(e, sl=sl):
                        ins = None
                        for hh in range(2):
                            ins = e.matmul(
                                p_nq[:, hh * 512:(hh + 1) * 512], ones1[:],
                                sqb[:, sl * 1024 + hh * 512: sl * 1024 + (hh + 1) * 512],
                                start=True, stop=True)
                        return ins
                    b.emit("tensor", mm_nq, waits=[("ac", w_qs)] + wz2,
                           incs=[("mm", 1)])
                    b.emit("vector", (lambda e, h=h: e.reduce_max(
                        mq2[0:1, h:h + 1], p_nq[:], axis=AX.X)),
                        waits=[("mm", b.at("mm"))], incs=[("ve", 1)])
            w_mq2 = b.at("ve")
        ctxA.close()
        ctxS = ExitStack()
        kf_sb = ctxS.enter_context(nc.sbuf_tensor("kf", [128, NKC * 128], BF16))
        vf_sb = ctxS.enter_context(nc.sbuf_tensor("vf", [128, NKC * 128], BF16))
        oall = ctxS.enter_context(nc.sbuf_tensor("oall", [128, H * T], BF16))
        et_sb = xbb
        rb_sb = tscr[:, 0:1024]
        negb = tscr[:, 1024:1028]
        nk2 = tscr[0:1, 1056:1072]
        rs2 = tscr[0:2, 1280:1792]
        with nc.psum_tensor(_nm("p_s0"), [128, 1024], F32) as p_s0, \
             nc.psum_tensor(_nm("p_s1"), [128, 1024], F32) as p_s1, \
             nc.psum_tensor(_nm("p_o"), [128, 1024], F32) as p_o, \
             nc.psum_tensor(_nm("p_sum2"), [2, 512], F32) as p_sum2, \
             nc.psum_tensor(_nm("p_rb"), [128, 512], F32) as p_rb:
            last_s_mm = 0
            last_pv_mm = 0
            pending_epi = [None]
            prev_exp_last = [0]
            prev_evac = [b.at("ve")]
            for g in range(HKV):
                if pending_epi[0] is not None:
                    pending_epi[0]()
                    pending_epi[0] = None
                kld_waits = [("cc", w_cc), ("mm", last_s_mm)]
                vld_waits = [("cc", w_cc), ("mm", last_pv_mm)]
                if g == 0:
                    kld_waits.append(("ve", w_qrot))
                    vld_waits.append(("ve", w_qrot))
                b.emit("sync", (lambda e, g=g: e.dma_start(
                    out=kf_sb[:, 0:4096].rearrange("p (r t) -> p r t", r=4),
                    in_=d_kout.rearrange("(r g p) t -> g p r t",
                                         r=NCORES, g=HKV)[g][:, 0:4, :])),
                    waits=kld_waits, incs=[("kvk", 16)])
                w_kld = b.at("kvk")
                b.emit("sync", (lambda e, g=g: e.dma_start(
                    out=kf_sb[:, 4096:8192].rearrange("p (r t) -> p r t", r=4),
                    in_=d_kout.rearrange("(r g p) t -> g p r t",
                                         r=NCORES, g=HKV)[g][:, 4:8, :])),
                    incs=[("kvk2", 16)])
                w_kld2 = b.at("kvk2")
                b.emit("sync", (lambda e, g=g: e.dma_start(
                    out=vf_sb[:, 0:4096].rearrange("p (kc hd) -> p kc hd", kc=32),
                    in_=d_vout.rearrange("(kc p) (g hd) -> g p kc hd",
                                         p=128, g=HKV)[g][:, 0:32, :])),
                    waits=vld_waits, incs=[("kvv", 16)])
                w_vld = b.at("kvv")
                b.emit("sync", (lambda e, g=g: e.dma_start(
                    out=vf_sb[:, 4096:8192].rearrange("p (kc hd) -> p kc hd", kc=32),
                    in_=d_vout.rearrange("(kc p) (g hd) -> g p kc hd",
                                         p=128, g=HKV)[g][:, 32:64, :])),
                    incs=[("kvv2", 16)])
                w_vld2 = b.at("kvv2")
                for c in range(8):
                    sl = c % 2
                    wz = [("mm", b.at("mm"))]
                    wkh = [("kvk", w_kld)] if c < 4 else [("kvk2", w_kld2)]
                    b.emit("scalar", (lambda e, c=c, sl=sl: e.activation(
                        out=sqb[:, sl * 1024:(sl + 1) * 1024],
                        in_=kf_sb[:, c * 1024:(c + 1) * 1024], func=AF.Square)),
                        waits=wkh + wz, incs=[("ac", 1)])
                    w_ks = b.at("ac")

                    def mm_nk(e, sl=sl):
                        ins = None
                        for hh in range(2):
                            ins = e.matmul(
                                p_o[0:1, hh * 512:(hh + 1) * 512], ones1[:],
                                sqb[:, sl * 1024 + hh * 512: sl * 1024 + (hh + 1) * 512],
                                start=True, stop=True)
                        return ins
                    b.emit("tensor", mm_nk,
                           waits=[("ac", w_ks), ("ve", b.at("ve"))], incs=[("mm", 1)])
                    if c == 0:
                        b.emit("vector", (lambda e, g=g: e.reduce_max(
                            nk2[0:1, g:g + 1], p_o[0:1, 0:1024], axis=AX.X)),
                            waits=[("mm", b.at("mm"))], incs=[("ve", 1)])
                    else:
                        b.emit("vector", lambda e: e.reduce_max(
                            nk2[0:1, 4:5], p_o[0:1, 0:1024], axis=AX.X),
                            waits=[("mm", b.at("mm"))], incs=[("ve", 1)])
                        b.emit("vector", (lambda e, g=g: e.tensor_max(
                            nk2[0:1, g:g + 1], nk2[0:1, g:g + 1], nk2[0:1, 4:5])),
                            waits=[("ve", b.at("ve"))], incs=[("ve", 1)])
                b.emit("vector", (lambda e, g=g: e.tensor_scalar(
                    nk2[0:1, 6:10], mq2[0:1, g * 4:(g + 1) * 4],
                    nk2[0:1, g:g + 1], 1.0 / 128.0, ALU.mult, ALU.mult)),
                    waits=[("ve", b.at("ve")), ("ve", w_mq2)], incs=[("ve", 1)])
                b.emit("scalar", lambda e: e.activation(
                    out=nk2[0:1, 6:10], in_=nk2[0:1, 6:10], func=AF.Ln),
                    waits=[("ve", b.at("ve")), ("ac", b.at("ac"))],
                    incs=[("ac", 1)])
                b.emit("scalar", lambda e: e.activation(
                    out=nk2[0:1, 6:10], in_=nk2[0:1, 6:10], func=AF.Exp, scale=0.5),
                    waits=[("ac", b.at("ac"))], incs=[("ac", 1)])
                b.emit("vector", lambda e: e.tensor_scalar_mul(
                    nk2[0:1, 6:10], nk2[0:1, 6:10], -1.0),
                    waits=[("ac", b.at("ac")), ("ve", b.at("ve"))],
                    incs=[("ve", 1)])
                w_bh = b.at("ve")
                b.emit("tensor", lambda e: e.matmul(
                    p_o[:, 0:HKV], onesf[:], nk2[0:1, 6:6 + HKV],
                    start=True, stop=True),
                    waits=[("ve", w_bh)], incs=[("mm", 1)])
                b.emit("scalar", lambda e: e.copy(out=negb, in_=p_o[:, 0:HKV]),
                       waits=[("mm", b.at("mm"))], incs=[("ac", 1)])
                w_negb = b.at("ac")
                for hh4 in range(4):
                    h = g * 4 + hh4
                    exp_done = {}
                    pv_done = {}

                    def mm_s(e, kc, psl, h):
                        ins = None
                        for hh in range(2):
                            ins = e.matmul(
                                psl[:, hh * 512:(hh + 1) * 512],
                                kf_sb[:, kc * 128:(kc + 1) * 128],
                                q_sb[:, h * T + hh * 512: h * T + (hh + 1) * 512],
                                start=True, stop=True)
                        return ins

                    def mm_pv(e, kc, esl):
                        ins = None
                        for hh in range(2):
                            ins = e.matmul(
                                p_o[:, hh * 512:(hh + 1) * 512],
                                vf_sb[:, kc * 128:(kc + 1) * 128],
                                et_sb[:, esl * T + hh * 512: esl * T + (hh + 1) * 512],
                                start=(kc == 0), stop=(kc == NKC - 1))
                            ins = e.matmul(
                                p_sum2[:], sumsel[:, hh * 2:(hh + 1) * 2],
                                et_sb[:, esl * T + hh * 512: esl * T + (hh + 1) * 512],
                                start=(kc == 0 and hh == 0),
                                stop=(kc == NKC - 1 and hh == 1))
                        return ins

                    def emit_s(kc):
                        psl = p_s0 if kc % 2 == 0 else p_s1
                        wz = [("ac", exp_done[kc - 2])] if kc >= 2 else \
                             [("ac", prev_exp_last[0])]
                        wkh = [("kvk", w_kld)] if kc < 32 else \
                              [("kvk2", w_kld2)]
                        b.emit("tensor",
                               (lambda e, kc=kc, psl=psl, h=h: mm_s(e, kc, psl, h)),
                               waits=wkh + [("ve", w_qrot)] + wz,
                               incs=[("mm", 1)])
                        w_smm = b.at("mm")
                        wz2 = [("mm", pv_done[kc - 2])] if kc >= 2 else []
                        b.emit("scalar", (lambda e, psl=psl, esl=kc % 2, hh4=hh4:
                               e.activation(
                                   out=et_sb[:, esl * T:(esl + 1) * T], in_=psl[:],
                                   func=AF.Exp, bias=negb[:, hh4:hh4 + 1],
                                   scale=ISQ)),
                               waits=[("mm", w_smm), ("ac", w_negb)] + wz2,
                               incs=[("ac", 1)])
                        exp_done[kc] = b.at("ac")
                        return w_smm

                    def emit_pv(kc):
                        wz = [("ve", prev_evac[0])] if kc == 0 else []
                        wvh = [("kvv", w_vld)] if kc < 32 else \
                              [("kvv2", w_vld2)]
                        b.emit("tensor", (lambda e, kc=kc, esl=kc % 2: mm_pv(e, kc, esl)),
                               waits=[("ac", exp_done[kc])] + wvh + wz,
                               incs=[("mm", 1)])
                        pv_done[kc] = b.at("mm")
                        return b.at("mm")

                    last_s_mm = emit_s(0)
                    last_s_mm = emit_s(1)
                    if pending_epi[0] is not None:
                        pending_epi[0]()
                        pending_epi[0] = None
                    last_pv_mm = emit_pv(0)
                    for kc in range(2, NKC):
                        last_s_mm = emit_s(kc)
                        last_pv_mm = emit_pv(kc - 1)
                    last_pv_mm = emit_pv(NKC - 1)

                    def make_epi(h=h, w_hd=last_pv_mm):
                        def _():
                            b.emit("vector", lambda e: e.reciprocal(rs2, p_sum2[:]),
                                   waits=[("mm", w_hd), ("ve", b.at("ve"))],
                                   incs=[("ve", 1)])
                            w_rs = b.at("ve")
                            for hh in range(2):
                                b.emit("tensor", (lambda e, hh=hh: e.matmul(
                                    p_rb[:], rowsel2[:, hh * 128:(hh + 1) * 128],
                                    rs2, start=True, stop=True)),
                                    waits=[("ve", w_rs), ("ac", b.at("ac"))],
                                    incs=[("mm", 1)])
                                b.emit("scalar", (lambda e, hh=hh: e.copy(
                                    out=rb_sb[:, hh * 512:(hh + 1) * 512],
                                    in_=p_rb[:])),
                                    waits=[("mm", b.at("mm"))], incs=[("ac", 1)])
                                w_cp = b.at("ac")
                                b.emit("vector", (lambda e, h=h, hh=hh: e.tensor_mul(
                                    oall[:, h * T + hh * 512: h * T + (hh + 1) * 512],
                                    p_o[:, hh * 512:(hh + 1) * 512],
                                    rb_sb[:, hh * 512:(hh + 1) * 512])),
                                    waits=[("ac", w_cp)], incs=[("ve", 1)])
                            prev_evac[0] = b.at("ve")
                        return _
                    pending_epi[0] = make_epi()
                    prev_exp_last[0] = exp_done[NKC - 1]
            if pending_epi[0] is not None:
                pending_epi[0]()
                pending_epi[0] = None
        w_sdpa = b.at("ve")
        with nc.psum_tensor(_nm("p_w0"), [128, 1024], F32) as p_w0, \
             nc.psum_tensor(_nm("p_w1"), [128, 1024], F32) as p_w1:
            wo_evac = {}
            for dt in range(DC):
                pw = p_w0 if dt % 2 == 0 else p_w1
                gi_g, slg, wzg, smg, tgg = sl_wg.acquire()
                b.emit("gpsimd", (lambda e, dt=dt, sl=slg: e.dma_start(
                    out=wgb[:, sl * 2048:(sl + 1) * 2048].rearrange(
                        "p (c i) -> p c i", c=DC),
                    in_=d_wo[:, dt * 128:(dt + 1) * 128].rearrange(
                        "(c p) i -> p c i", p=128))),
                    waits=wzg, incs=[(smg, 16)])
                wz2 = [("ve", wo_evac[dt - 2])] if dt >= 2 else []

                def mm_wo(e, sl=slg, pw=pw):
                    ins = None
                    for hc in range(H):
                        for hh in range(2):
                            ins = e.matmul(
                                pw[:, hh * 512:(hh + 1) * 512],
                                wgb[:, sl * 2048 + hc * 128: sl * 2048 + (hc + 1) * 128],
                                oall[:, hc * T + hh * 512: hc * T + (hh + 1) * 512],
                                start=(hc == 0), stop=(hc == H - 1))
                    return ins
                b.emit("tensor", mm_wo,
                       waits=[(smg, tgg), ("ve", w_sdpa)] + wz2, incs=[("mm", 1)])
                sl_wg.consumed(gi_g, b.at("mm"))
                w_womm = b.at("mm")
                b.emit("vector", (lambda e, dt=dt, pw=pw: e.tensor_add(
                    x_sb[:, dt * T:(dt + 1) * T], x_sb[:, dt * T:(dt + 1) * T],
                    pw[:])),
                    waits=[("mm", w_womm)], incs=[("ve", 1)])
                wo_evac[dt] = b.at("ve")
        attn_evac = dict(wo_evac)
        ctxS.close()
        ctxQ.close()
    else:
        ctxA.close()

    if stage >= 3:
        ctxF = ExitStack()
        y2_sb = ctxF.enter_context(nc.sbuf_tensor("y2_sb", [128, DC * T], BF16))
        fscr2 = ctxF.enter_context(nc.sbuf_tensor("fscr2", [128, 3 * 1024], F32,
                                                  side="right"))
        h_sb2 = ctxF.enter_context(nc.sbuf_tensor("h_sb2", [128, EIT * T], BF16,
                                                  side="right"))
        wdb2 = ctxF.enter_context(nc.sbuf_tensor("wdb2", [128, 4 * EIT * 128],
                                                 BF16, side="right"))
        w_y3, _ = emit_ln(y2_sb, cw=lambda c: [("ve", attn_evac[c])])
        w_router3 = emit_router(y2_sb, w_y3)
        emit_ffn(y2_sb, fscr2[:, 0:1024], fscr2[:, 1024:2048],
                 fscr2[:, 2048:3072], h_sb2, wdb2, w_router3)
        ctxF.close()

    b.emit("sync", lambda e: e.dma_start(
        out=d_out.rearrange("(k p) t -> p k t", p=128),
        in_=x_sb[:].rearrange("p (k t) -> p k t", k=DC)),
        waits=[("ve", b.at("ve")), ("mm", b.at("mm")), ("ac", b.at("ac"))],
        incs=[("st", 16)])
    w_out = b.at("st")
    b.emit("sync", (lambda e, w=w_out: e.wait_ge(sems["st"], w)))

    with nc.Block() as block:
        def mk_runner(eng_ops):
            def run(eng):
                for waits, fn, incs in eng_ops:
                    for s, v in waits:
                        eng.wait_ge(sems[s], v)
                    ins = fn(eng)
                    if incs:
                        assert ins is not None, "then_inc needs an instruction"
                        for s, a in incs:
                            ins = ins.then_inc(sems[s], a)
            return run
        block.sync(mk_runner(b.ops["sync"]))
        block.gpsimd(mk_runner(b.ops["gpsimd"]))
        block.tensor(mk_runner(b.ops["tensor"]))
        block.vector(mk_runner(b.ops["vector"]))
        block.scalar(mk_runner(b.ops["scalar"]))

    es.close()
    return nc


def kernel(L0_attn_x, L0_attn_Wq, L0_attn_Wk, L0_attn_Wv, L0_attn_Wo,
           L0_ffn_router_w, L0_ffn_w_gate, L0_ffn_w_up, L0_ffn_w_down,
           L0_ffn_sw_gate, L0_ffn_sw_up, L0_ffn_sw_down, L0_ffn_shared_gate_w):
    x = np.asarray(L0_attn_x)
    assert x.shape == (1, 8192, 2048)
    xt = np.ascontiguousarray(x[0].T.astype(np.float32))

    wq = np.ascontiguousarray(np.asarray(L0_attn_Wq, dtype=np.float32))
    wk = np.ascontiguousarray(np.asarray(L0_attn_Wk, dtype=np.float32))
    wv = np.ascontiguousarray(np.asarray(L0_attn_Wv, dtype=np.float32))
    wo = np.ascontiguousarray(np.asarray(L0_attn_Wo, dtype=np.float32))
    rwT = np.ascontiguousarray(np.asarray(L0_ffn_router_w, dtype=np.float32).T)
    sgw = np.ascontiguousarray(
        np.asarray(L0_ffn_shared_gate_w, dtype=np.float32).reshape(16, 128).T)
    wg = np.ascontiguousarray(np.asarray(L0_ffn_w_gate, dtype=np.float32))
    wu = np.ascontiguousarray(np.asarray(L0_ffn_w_up, dtype=np.float32))
    wd = np.ascontiguousarray(np.asarray(L0_ffn_w_down, dtype=np.float32))
    swg = np.ascontiguousarray(np.asarray(L0_ffn_sw_gate, dtype=np.float32))
    swu = np.ascontiguousarray(np.asarray(L0_ffn_sw_up, dtype=np.float32))
    swd = np.ascontiguousarray(np.asarray(L0_ffn_sw_down, dtype=np.float32))

    inv = 1.0 / (10000.0 ** (np.arange(0, 128, 2, dtype=np.float64) / 128.0))
    pos = np.arange(8192, dtype=np.float64)
    freqs = np.concatenate([inv, inv])
    emb = freqs[:, None] * pos[None, :]
    cos_t = np.cos(emb).astype(np.float32)
    sin_t = np.sin(emb).astype(np.float32)

    ones1 = np.ones((128, 1), np.float32)
    onesf = np.ones((1, 128), np.float32)
    halff = np.full((1, 128), 0.5, np.float32)
    identb = np.eye(128, dtype=np.float32)
    rollm = np.zeros((128, 128), np.float32)
    rollm[(np.arange(128) - 1) % 128, np.arange(128)] = 1.0
    sumsel = np.zeros((128, 4), np.float32)
    sumsel[:, 0] = 1.0; sumsel[:, 3] = 1.0
    rowsel2 = np.zeros((2, 256), np.float32)
    rowsel2[0, 0:128] = 1.0; rowsel2[1, 128:256] = 1.0
    sel05 = np.zeros((8, 8 * 128), np.float32)
    for e in range(8):
        sel05[e, e * 128:(e + 1) * 128] = 0.5

    nc = build_program(STAGE)
    in_maps = []
    for c in range(NCORES):
        sl = slice(c * T, (c + 1) * T)
        in_maps.append({
            "x": np.ascontiguousarray(xt[:, sl]),
            "cos_t": np.ascontiguousarray(cos_t[:, sl]),
            "sin_t": np.ascontiguousarray(sin_t[:, sl]),
            "wq": wq, "wk": wk, "wv": wv, "wo": wo, "rollm": rollm,
            "rwT": rwT, "sgw": sgw,
            "wg": wg, "wu": wu, "wd": wd,
            "swg": swg, "swu": swu, "swd": swd,
            "ones1": ones1, "onesf": onesf, "halff": halff,
            "identb": identb, "sel05": sel05, "sumsel": sumsel,
            "rowsel2": rowsel2,
        })
    res = run_bass_kernel_spmd(nc, in_maps, core_ids=list(range(NCORES)))
    out = np.concatenate([r["out"] for r in res.results], axis=1)
    return np.ascontiguousarray(out.T).reshape(1, 8192, 2048).astype(np.float32)


# version 16
# speedup vs baseline: 1.0083x; 1.0083x over previous
import numpy as np
from contextlib import ExitStack

import concourse.bass as bass
import concourse.mybir as mybir
from concourse.bass_utils import run_bass_kernel_spmd

F32 = mybir.dt.float32
BF16 = mybir.dt.bfloat16
AF = mybir.ActivationFunctionType
ALU = mybir.AluOpType
AX = mybir.AxisListType

NCORES = 8
T = 1024
D = 2048
DC = 16
H = 16
HKV = 4
HD = 128
NE = 8
EI = 1408
EIT = 11
NSU = 12
NKC = 64
ISQ = float(1.0 / np.sqrt(128.0))
EPS = 1e-5

STAGE = 3

ENGS = ("sync", "gpsimd", "tensor", "vector", "scalar")


class Build:
    def __init__(self):
        self.ops = {e: [] for e in ENGS}
        self.cnt = {}

    def at(self, s):
        return self.cnt.get(s, 0)

    def emit(self, eng, fn, waits=(), incs=()):
        waits = [(s, v) for (s, v) in waits if v > 0]
        for s, a in incs:
            self.cnt[s] = self.cnt.get(s, 0) + a
        self.ops[eng].append((waits, fn, list(incs)))


class Slots:

    def __init__(self, n=2, prefix="s"):
        self.n = n
        self.prefix = prefix
        self.i = 0
        self.credit = {}
        self.uses = [0] * n

    def acquire(self):
        gi = self.i
        self.i += 1
        sl = gi % self.n
        wz = [("mm", self.credit[gi - self.n])] \
            if (gi - self.n) in self.credit else []
        self.uses[sl] += 1
        return gi, sl, wz, "%s%d" % (self.prefix, sl), 16 * self.uses[sl]

    def consumed(self, gi, cnt):
        self.credit[gi] = cnt


def build_program(stage):
    nc = bass.Bass("TRN2", target_bir_lowering=False)
    b = Build()

    ap = lambda name, shape, dt=F32, kind="ExternalInput": nc.dram_tensor(
        name, shape, dt, kind=kind).ap()
    d_x = ap("x", [D, T])
    d_cos = ap("cos_t", [HD, T])
    d_sin = ap("sin_t", [HD, T])
    d_wq = ap("wq", [D, D])
    d_wk = ap("wk", [D, HKV * HD])
    d_rollm = ap("rollm", [128, 128])
    d_wv = ap("wv", [D, HKV * HD])
    d_wo = ap("wo", [D, D])
    d_rwT = ap("rwT", [D, NE])
    d_sgw = ap("sgw", [128, DC])
    d_wg = ap("wg", [NE, D, EI]); d_wu = ap("wu", [NE, D, EI])
    d_wd = ap("wd", [NE, EI, D])
    d_swg = ap("swg", [D, 4 * EI]); d_swu = ap("swu", [D, 4 * EI])
    d_swd = ap("swd", [4 * EI, D])
    d_ones1 = ap("ones1", [128, 1])
    d_onesf = ap("onesf", [1, 128])
    d_halff = ap("halff", [1, 128])
    d_identb = ap("identb", [128, 128])
    d_sel = ap("sel05", [NE, NE * 128])
    d_sumsel = ap("sumsel", [128, 4])
    d_rowsel2 = ap("rowsel2", [2, 256])
    d_out = ap("out", [D, T], F32, "ExternalOutput")
    d_kin = nc.dram_tensor("kin", [HKV * HD, T], BF16, kind="Internal").ap()
    d_kout = nc.dram_tensor("kout", [NCORES * HKV * HD, T], BF16, kind="Internal").ap()
    d_vin = nc.dram_tensor("vin", [T, HKV * HD], BF16, kind="Internal").ap()
    d_vout = nc.dram_tensor("vout", [NCORES * T, HKV * HD], BF16, kind="Internal").ap()
    d_nrmin = nc.dram_tensor("nrmin", [1, 8], F32, kind="Internal").ap()
    d_nrmout = nc.dram_tensor("nrmout", [8, 8], F32, kind="Internal").ap()

    es = ExitStack()
    sb = lambda name, shape, dt, side="left": es.enter_context(
        nc.sbuf_tensor(name, shape, dt, side=side))

    x_sb = sb("x_sb", [128, DC * T], F32)
    ones1 = sb("ones1_s", [128, 1], BF16)
    onesf = sb("onesf_s", [1, 128], F32)
    halff = sb("halff_s", [1, 128], F32)
    identb = sb("identb_s", [128, 128], BF16)
    rollm_sb = sb("rollm_s", [128, 128], BF16)
    sel05 = sb("sel05_s", [NE, NE * 128], BF16)
    rwT_sb = sb("rwT_sb", [128, DC * NE], BF16)
    sgw_sb = sb("sgw_sb", [128, DC], BF16)
    sumsel = sb("sumsel_s", [128, 4], BF16)
    rowsel2 = sb("rowsel2_s", [2, 256], F32)
    rows = sb("rows", [1, 2 * 1024], F32)
    wgb = sb("wgb", [128, 2 * DC * 128], BF16, "right")
    wub = sb("wub", [128, 2 * DC * 128], BF16, "right")
    sqb = sb("sqb", [128, 2 * 1024], BF16, "right")
    xbb = sb("xbb", [128, 2 * 1024], BF16, "right")
    tscr = sb("tscr", [128, 2 * 1024], F32, "right")
    tscr_eq = sb("tscr_eq", [128, NE], F32, "right")
    lg_sb = sb("lg_sb", [128, 8 * NE], F32, "right")
    comb_sb = sb("comb_sb", [128, 8 * NE], BF16, "right")
    top8 = sb("top8", [128, 8 * NE], F32, "right")
    td = sb("td", [128, 8], F32, "right")
    w1s = sb("w1s", [128, 8], F32, "right")
    w2s = sb("w2s", [128, 8], F32, "right")
    combT_sb = sb("combT_sb", [NE, 1024], BF16, "right")

    sems = {}
    for s in ("sy", "syc", "gwc", "kvk", "kvv", "kvn", "cc", "st", "mm", "ac",
              "ve",
              "g0", "g1", "u0", "u1", "d0", "d1", "d2", "d3", "v0", "v1"):
        sems[s] = es.enter_context(nc.semaphore(s))

    sl_wg = Slots(2, "g")
    sl_wu = Slots(2, "u")
    sl_wd = Slots(4, "d")
    sl_wv = Slots(2, "v")
    _uid = [0]

    def _nm(base):
        _uid[0] += 1
        return "%s_%d" % (base, _uid[0])

    b.emit("sync", lambda e: e.dma_start(
        out=x_sb[:].rearrange("p (k t) -> p k t", k=DC),
        in_=d_x.rearrange("(k p) t -> p k t", p=128)), incs=[("sy", 16)])
    w_x = b.at("sy")
    for out_ap, in_ap in (
            (ones1[:], d_ones1[:]),
            (identb[:], d_identb[:]),
            (rollm_sb[:], d_rollm[:]),
            (sel05[:], d_sel[:]),
            (rwT_sb[:].rearrange("p (c e) -> p c e", c=DC),
             d_rwT.rearrange("(c p) e -> p c e", p=128)),
            (sgw_sb[:], d_sgw[:]),
            (sumsel[:], d_sumsel[:])):
        b.emit("gpsimd", (lambda e, o=out_ap, i=in_ap: e.dma_start(out=o, in_=i)),
               incs=[("gwc", 16)])
    w_consts_gw = None
    b.emit("sync", lambda e: e.dma_start(out=onesf[:], in_=d_onesf[:]),
           incs=[("syc", 16)])
    b.emit("sync", lambda e: e.dma_start(out=halff[:], in_=d_halff[:]),
           incs=[("syc", 16)])
    b.emit("sync", lambda e: e.dma_start(out=rowsel2[:], in_=d_rowsel2[:]),
           incs=[("syc", 16)])
    w_consts_sy = b.at("syc")

    def emit_ln(y_sb, cw=None):
        if cw is None:
            cw = lambda c: [("sy", w_x)]
        with nc.psum_tensor(_nm("p_sx"), [1, 1024], F32) as p_sx, \
             nc.psum_tensor(_nm("p_sq"), [1, 1024], F32) as p_sq:
            sq_slots = [sqb[:, 0:1024], sqb[:, 1024:2048], y_sb[:, 0:1024]]
            xb_slots = [xbb[:, 0:1024], xbb[:, 1024:2048], y_sb[:, 1024:2048]]
            pe_done = {}
            NSL = 2
            for c in range(DC):
                sl = c % NSL
                wpe = [("mm", pe_done[c - NSL])] if c >= NSL else []
                wx0s = ([("gwc", w_consts_gw)] if c == 0 else [])
                b.emit("scalar", (lambda e, c=c, sl=sl: e.activation(
                    out=sq_slots[sl], in_=x_sb[:, c * T:(c + 1) * T],
                    func=AF.Square)),
                    waits=wpe + wx0s + cw(c), incs=[("ac", 1)])
                w_sq = b.at("ac")
                b.emit("vector", (lambda e, c=c, sl=sl: e.tensor_copy(
                    out=xb_slots[sl], in_=x_sb[:, c * T:(c + 1) * T])),
                    waits=wpe + cw(c), incs=[("ve", 1)])
                w_xb = b.at("ve")

                def mm_sums(e, c=c, sl=sl):
                    ins = None
                    for src, p in ((xb_slots[sl], p_sx), (sq_slots[sl], p_sq)):
                        for hh in range(2):
                            ins = e.matmul(
                                p[:, hh * 512:(hh + 1) * 512], ones1[:],
                                src[:, hh * 512:(hh + 1) * 512],
                                start=(c == 0), stop=(c == DC - 1))
                    return ins
                b.emit("tensor", mm_sums, waits=[("ac", w_sq), ("ve", w_xb)],
                       incs=[("mm", 1)])
                pe_done[c] = b.at("mm")
            w_sums = b.at("mm")
            m_row = rows[0:1, 0:1024]
            aux = rows[0:1, 1024:2048]
            mr_row = m_row
            t_row = tscr[0:1, 0:1024]
            b.emit("vector", lambda e: e.tensor_scalar_mul(m_row, p_sx[:], 1.0 / D),
                   waits=[("mm", w_sums), ("ve", b.at("ve"))], incs=[("ve", 1)])
            b.emit("vector", lambda e: e.tensor_scalar_mul(aux, p_sq[:], 1.0 / D),
                   waits=[("ve", b.at("ve"))], incs=[("ve", 1)])
            b.emit("vector", lambda e: e.tensor_mul(t_row, m_row, m_row),
                   waits=[("ve", b.at("ve")), ("ac", b.at("ac"))], incs=[("ve", 1)])
            b.emit("vector", lambda e: e.tensor_sub(aux, aux, t_row),
                   waits=[("ve", b.at("ve"))], incs=[("ve", 1)])
            w_var = b.at("ve")
            b.emit("vector", lambda e: e.tensor_scalar_add(aux, aux, EPS),
                   waits=[("ve", w_var)], incs=[("ve", 1)])
            b.emit("scalar", lambda e: e.activation(out=aux, in_=aux, func=AF.Ln),
                   waits=[("ve", b.at("ve"))], incs=[("ac", 1)])
            b.emit("scalar", lambda e: e.activation(out=aux, in_=aux, func=AF.Exp,
                                                    scale=-0.5),
                   waits=[("ac", b.at("ac"))], incs=[("ac", 1)])
            w_r = b.at("ac")
            b.emit("vector", lambda e: e.tensor_mul(mr_row, m_row, aux),
                   waits=[("ac", w_r), ("ve", b.at("ve"))], incs=[("ve", 1)])
            w_mr = b.at("ve")
        with nc.psum_tensor(_nm("p_rb"), [128, 1024], F32) as p_rb, \
             nc.psum_tensor(_nm("p_mb"), [128, 1024], F32) as p_mb:
            def mm_bc(e):
                ins = None
                for hh in range(2):
                    ins = e.matmul(p_rb[:, hh * 512:(hh + 1) * 512], onesf[:],
                                   aux[0:1, hh * 512:(hh + 1) * 512],
                                   start=True, stop=True)
                    ins = e.matmul(p_mb[:, hh * 512:(hh + 1) * 512], onesf[:],
                                   mr_row[0:1, hh * 512:(hh + 1) * 512],
                                   start=True, stop=True)
                return ins
            b.emit("tensor", mm_bc, waits=[("ve", w_mr), ("syc", w_consts_sy)],
                   incs=[("mm", 1)])
            w_bc = b.at("mm")
            b.emit("scalar", lambda e: e.copy(out=tscr[:, 0:1024], in_=p_rb[:]),
                   waits=[("mm", w_bc)], incs=[("ac", 1)])
            b.emit("scalar", lambda e: e.copy(out=sqb[:, 0:1024], in_=p_mb[:]),
                   waits=[("ac", b.at("ac"))], incs=[("ac", 1)])
            w_cp = b.at("ac")
            y_done = []
            for c in range(DC):
                b.emit("vector", (lambda e, c=c: e.tensor_mul(
                    y_sb[:, c * T:(c + 1) * T], x_sb[:, c * T:(c + 1) * T],
                    tscr[:, 0:1024])),
                    waits=([("ac", w_cp)] if c == 0 else []), incs=[("ve", 1)])
                b.emit("vector", (lambda e, c=c: e.tensor_sub(
                    y_sb[:, c * T:(c + 1) * T], y_sb[:, c * T:(c + 1) * T],
                    sqb[:, 0:1024])),
                    waits=[("ve", b.at("ve"))], incs=[("ve", 1)])
                y_done.append(b.at("ve"))
        return b.at("ve"), y_done

    def emit_router(y_sb, w_y):
        with nc.psum_tensor(_nm("p_lg0"), [128, NE], F32) as p_lg0, \
             nc.psum_tensor(_nm("p_lg1"), [128, NE], F32) as p_lg1:
            for tc in range(8):
                p_lg = p_lg0 if tc % 2 == 0 else p_lg1
                wz = [("ac", b.at("ac"))] if tc >= 2 else []

                def mm_lg(e, tc=tc, p_lg=p_lg):
                    ins = None
                    for c in range(DC):
                        ins = e.matmul(
                            p_lg[:], y_sb[:, c * T + tc * 128: c * T + (tc + 1) * 128],
                            rwT_sb[:, c * NE:(c + 1) * NE],
                            start=(c == 0), stop=(c == DC - 1))
                    return ins
                b.emit("tensor", mm_lg, waits=[("ve", w_y)] + wz, incs=[("mm", 1)])
                b.emit("scalar", (lambda e, tc=tc, p_lg=p_lg: e.copy(
                    out=lg_sb[:, tc * NE:(tc + 1) * NE], in_=p_lg[:])),
                    waits=[("mm", b.at("mm"))], incs=[("ac", 1)])
        w_lg = b.at("ac")
        with nc.psum_tensor(_nm("p_sg"), [1, 1024], F32) as p_sg:
            def mm_sg(e):
                ins = None
                for hh in range(2):
                    for c in range(DC):
                        ins = e.matmul(
                            p_sg[:, hh * 512:(hh + 1) * 512], sgw_sb[:, c:c + 1],
                            y_sb[:, c * T + hh * 512: c * T + (hh + 1) * 512],
                            start=(c == 0), stop=(c == DC - 1))
                return ins
            b.emit("tensor", mm_sg, waits=[("ve", w_y), ("ac", w_lg)],
                   incs=[("mm", 1)])
            b.emit("scalar", lambda e: e.activation(out=rows[0:1, 1024:2048],
                                                    in_=p_sg[:], func=AF.Sigmoid),
                   waits=[("mm", b.at("mm"))], incs=[("ac", 1)])
        for tc in range(8):
            b.emit("vector", (lambda e, tc=tc: e.max(
                out=top8[:, tc * NE:(tc + 1) * NE],
                in_=lg_sb[:, tc * NE:(tc + 1) * NE])),
                waits=[("ac", w_lg)] + ([("ve", b.at("ve"))] if tc == 0 else []),
                incs=[("ve", 1)])
        w_max = b.at("ve")
        for tc in range(8):
            b.emit("vector", (lambda e, tc=tc: e.tensor_sub(
                td[:, tc:tc + 1], top8[:, tc * NE:tc * NE + 1],
                top8[:, tc * NE + 1:tc * NE + 2])),
                waits=[("ve", w_max)] if tc == 0 else [], incs=[("ve", 1)])
        w_td = b.at("ve")
        b.emit("scalar", lambda e: e.activation(out=w1s[:], in_=td[:],
                                                func=AF.Sigmoid),
               waits=[("ve", w_td), ("ac", b.at("ac"))], incs=[("ac", 1)])
        b.emit("scalar", lambda e: e.activation(out=w2s[:], in_=td[:],
                                                func=AF.Sigmoid, scale=-1.0),
               waits=[("ac", b.at("ac"))], incs=[("ac", 1)])
        w_sig = b.at("ac")
        for tc in range(8):
            b.emit("vector", (lambda e, tc=tc: e.tensor_scalar(
                comb_sb[:, tc * NE:(tc + 1) * NE], lg_sb[:, tc * NE:(tc + 1) * NE],
                top8[:, tc * NE:tc * NE + 1], None, ALU.is_equal)),
                waits=[("ve", b.at("ve"))], incs=[("ve", 1)])
            b.emit("vector", (lambda e, tc=tc: e.tensor_scalar(
                tscr_eq[:], lg_sb[:, tc * NE:(tc + 1) * NE],
                top8[:, tc * NE + 1:tc * NE + 2], None, ALU.is_equal)),
                waits=[("ve", b.at("ve"))], incs=[("ve", 1)])
            b.emit("vector", (lambda e, tc=tc: e.tensor_sub(
                tscr_eq[:], tscr_eq[:], comb_sb[:, tc * NE:(tc + 1) * NE])),
                waits=[("ve", b.at("ve"))], incs=[("ve", 1)])
            b.emit("vector", lambda e: e.tensor_scalar_max(tscr_eq[:], tscr_eq[:], 0.0),
                   waits=[("ve", b.at("ve"))], incs=[("ve", 1)])
            b.emit("vector", (lambda e, tc=tc: e.tensor_scalar(
                comb_sb[:, tc * NE:(tc + 1) * NE], comb_sb[:, tc * NE:(tc + 1) * NE],
                w1s[:, tc:tc + 1], None, ALU.mult)),
                waits=[("ac", w_sig), ("ve", b.at("ve"))], incs=[("ve", 1)])
            b.emit("vector", (lambda e, tc=tc: e.tensor_scalar(
                tscr_eq[:], tscr_eq[:], w2s[:, tc:tc + 1], None, ALU.mult)),
                waits=[("ve", b.at("ve"))], incs=[("ve", 1)])
            b.emit("vector", (lambda e, tc=tc: e.tensor_add(
                comb_sb[:, tc * NE:(tc + 1) * NE], comb_sb[:, tc * NE:(tc + 1) * NE],
                tscr_eq[:])),
                waits=[("ve", b.at("ve"))], incs=[("ve", 1)])
        w_comb = b.at("ve")
        with nc.psum_tensor(_nm("p_tr0"), [NE, 128], BF16) as p_tr0, \
             nc.psum_tensor(_nm("p_tr1"), [NE, 128], BF16) as p_tr1:
            for tc in range(8):
                p_tr = p_tr0 if tc % 2 == 0 else p_tr1
                wz = [("ac", b.at("ac"))] if tc >= 2 else []
                b.emit("tensor", (lambda e, tc=tc, p_tr=p_tr: e.transpose(
                    p_tr[:], comb_sb[:, tc * NE:(tc + 1) * NE], identb[:])),
                    waits=[("ve", w_comb), ("gwc", w_consts_gw)] + wz,
                    incs=[("mm", 1)])
                b.emit("scalar", (lambda e, tc=tc, p_tr=p_tr: e.copy(
                    out=combT_sb[:, tc * 128:(tc + 1) * 128], in_=p_tr[:])),
                    waits=[("mm", b.at("mm"))], incs=[("ac", 1)])
        return b.at("ac")

    def emit_ffn(y_sb, sG, tmp, cbc_sb, h_sb, wdb, w_router):
        floor_mm = b.at("mm")
        floor_ve = b.at("ve")
        nload = [0]
        with nc.psum_tensor(_nm("p_g"), [128, 1024], F32) as p_g, \
             nc.psum_tensor(_nm("p_u"), [128, 1024], F32) as p_u, \
             nc.psum_tensor(_nm("p_d0"), [128, 1024], F32) as p_d0, \
             nc.psum_tensor(_nm("p_d1"), [128, 1024], F32) as p_d1:
            dn_evac = {}
            last_cbc_use = 0
            last_down_mm = 0
            for su in range(NSU):
                is_exp = su < NE
                if is_exp:
                    def mk_cb(e, su=su):
                        ins = None
                        for hh in range(2):
                            ins = e.matmul(
                                p_u[:, hh * 512:(hh + 1) * 512],
                                sel05[:, su * 128:(su + 1) * 128],
                                combT_sb[0:NE, hh * 512:(hh + 1) * 512],
                                start=True, stop=True)
                        return ins
                else:
                    def mk_cb(e, su=su):
                        ins = None
                        for hh in range(2):
                            ins = e.matmul(
                                p_u[:, hh * 512:(hh + 1) * 512], halff[:],
                                rows[0:1, 1024 + hh * 512: 1024 + (hh + 1) * 512],
                                start=True, stop=True)
                        return ins
                b.emit("tensor", mk_cb,
                       waits=[("ac", w_router), ("ve", b.at("ve"))], incs=[("mm", 1)])
                b.emit("scalar", lambda e: e.copy(out=cbc_sb[:], in_=p_u[:]),
                       waits=[("mm", b.at("mm")), ("ve", last_cbc_use)],
                       incs=[("ac", 1)])
                w_cbc = b.at("ac")
                h_ready_after = last_down_mm
                h_done = {}
                for j in range(EIT):
                    gi_g, slg, wzg, smg, tgg = sl_wg.acquire()
                    gi_u, slu, wzu, smu, tgu = sl_wu.acquire()
                    if is_exp:
                        src_g = d_wg[su][:, j * 128:(j + 1) * 128]
                        src_u = d_wu[su][:, j * 128:(j + 1) * 128]
                    else:
                        g4 = su - NE
                        src_g = d_swg[:, g4 * EI + j * 128: g4 * EI + (j + 1) * 128]
                        src_u = d_swu[:, g4 * EI + j * 128: g4 * EI + (j + 1) * 128]
                    b.emit("gpsimd", (lambda e, sl=slg, s=src_g: e.dma_start(
                        out=wgb[:, sl * 2048:(sl + 1) * 2048].rearrange(
                            "p (c i) -> p c i", c=DC),
                        in_=s.rearrange("(c p) i -> p c i", p=128))),
                        waits=wzg, incs=[(smg, 16)])
                    assert b.at(smg) == tgg
                    b.emit("gpsimd", (lambda e, sl=slu, s=src_u: e.dma_start(
                        out=wub[:, sl * 2048:(sl + 1) * 2048].rearrange(
                            "p (c i) -> p c i", c=DC),
                        in_=s.rearrange("(c p) i -> p c i", p=128))),
                        waits=wzu, incs=[(smu, 16)])
                    assert b.at(smu) == tgu

                    def mm_g(e, sl=slg):
                        ins = None
                        for c in range(DC):
                            for hh in range(2):
                                ins = e.matmul(
                                    p_g[:, hh * 512:(hh + 1) * 512],
                                    wgb[:, sl * 2048 + c * 128: sl * 2048 + (c + 1) * 128],
                                    y_sb[:, c * T + hh * 512: c * T + (hh + 1) * 512],
                                    start=(c == 0), stop=(c == DC - 1))
                        return ins
                    b.emit("tensor", mm_g,
                           waits=[(smg, tgg), ("ac", b.at("ac"))], incs=[("mm", 1)])
                    sl_wg.consumed(gi_g, b.at("mm"))
                    w_gmm = b.at("mm")

                    def mm_u(e, sl=slu):
                        ins = None
                        for c in range(DC):
                            for hh in range(2):
                                ins = e.matmul(
                                    p_u[:, hh * 512:(hh + 1) * 512],
                                    wub[:, sl * 2048 + c * 128: sl * 2048 + (c + 1) * 128],
                                    y_sb[:, c * T + hh * 512: c * T + (hh + 1) * 512],
                                    start=(c == 0), stop=(c == DC - 1))
                        return ins
                    b.emit("tensor", mm_u,
                           waits=[(smu, tgu), ("ac", w_cbc), ("ve", b.at("ve"))],
                           incs=[("mm", 1)])
                    sl_wu.consumed(gi_u, b.at("mm"))
                    w_umm = b.at("mm")
                    b.emit("scalar", lambda e: e.activation(out=sG[:], in_=p_g[:],
                                                            func=AF.Silu),
                           waits=[("mm", w_gmm), ("ve", b.at("ve"))], incs=[("ac", 1)])
                    w_silu = b.at("ac")
                    b.emit("vector", lambda e: e.tensor_mul(tmp[:], sG[:], cbc_sb[:]),
                           waits=[("ac", w_silu)], incs=[("ve", 1)])
                    hw = [("ve", b.at("ve")), ("mm", w_umm)]
                    if j == 0:
                        hw.append(("mm", h_ready_after))
                    b.emit("vector", (lambda e, j=j: e.tensor_mul(
                        h_sb[:, j * T:(j + 1) * T], tmp[:], p_u[:])),
                        waits=hw, incs=[("ve", 1)])
                    last_cbc_use = b.at("ve")
                    h_done[j] = b.at("ve")
                w_h = b.at("ve")
                for dt in range(DC):
                    pd = p_d0 if dt % 2 == 0 else p_d1
                    gi_d, sld, wzd, smd, tgd = sl_wd.acquire()
                    if is_exp:
                        src_d = d_wd[su][:, dt * 128:(dt + 1) * 128]
                    else:
                        g4 = su - NE
                        src_d = d_swd[g4 * EI:(g4 + 1) * EI, dt * 128:(dt + 1) * 128]
                    if nload[0] < 4:
                        wzd = wzd + [("mm", floor_mm), ("ve", floor_ve)]
                        nload[0] += 1
                    b.emit("gpsimd", (lambda e, sl=sld, s=src_d: e.dma_start(
                        out=wdb[:, sl * EIT * 128:(sl + 1) * EIT * 128].rearrange(
                            "p (i d) -> p i d", i=EIT),
                        in_=s.rearrange("(i p) d -> p i d", p=128))),
                        waits=wzd, incs=[(smd, 16)])
                    assert b.at(smd) == tgd
                    wz2 = [("ve", dn_evac[dt - 2])] if dt >= 2 else \
                          [("ve", b.at("ve") if su == 0 else dn_evac.get(DC - 2 + (dt % 2), 0))]

                    def mm_d(e, sl=sld, pd=pd, irange=None):
                        ins = None
                        for hh in range(2):
                            for i in irange:
                                ins = e.matmul(
                                    pd[:, hh * 512:(hh + 1) * 512],
                                    wdb[:, sl * EIT * 128 + i * 128: sl * EIT * 128 + (i + 1) * 128],
                                    h_sb[:, i * T + hh * 512: i * T + (hh + 1) * 512],
                                    start=(i == 0), stop=(i == EIT - 1))
                        return ins
                    if dt < 2:
                        b.emit("tensor",
                               (lambda e, sl=sld, pd=pd: mm_d(e, sl, pd,
                                                              range(EIT - 1))),
                               waits=[(smd, tgd), ("ve", h_done[EIT - 2])] + wz2,
                               incs=[("mm", 1)])
                        b.emit("tensor",
                               (lambda e, sl=sld, pd=pd: mm_d(e, sl, pd,
                                                              [EIT - 1])),
                               waits=[("ve", w_h)], incs=[("mm", 1)])
                    else:
                        b.emit("tensor",
                               (lambda e, sl=sld, pd=pd: mm_d(e, sl, pd,
                                                              range(EIT))),
                               waits=[(smd, tgd), ("ve", w_h)] + wz2,
                               incs=[("mm", 1)])
                    sl_wd.consumed(gi_d, b.at("mm"))
                    last_down_mm = b.at("mm")
                    w_dmm = b.at("mm")
                    b.emit("vector", (lambda e, dt=dt, pd=pd: e.tensor_add(
                        x_sb[:, dt * T:(dt + 1) * T], x_sb[:, dt * T:(dt + 1) * T],
                        pd[:])),
                        waits=[("mm", w_dmm)], incs=[("ve", 1)])
                    dn_evac[dt] = b.at("ve")
        return dict(dn_evac)

    ctxA = ExitStack()
    cos_sb = ctxA.enter_context(nc.sbuf_tensor("cos_sb", [128, 2 * T], BF16))
    y_sb = ctxA.enter_context(nc.sbuf_tensor("y_sb", [128, DC * T], BF16))
    b.emit("gpsimd", lambda e: e.dma_start(out=cos_sb[:, 0:T], in_=d_cos[:]),
           incs=[("gwc", 16)])
    b.emit("gpsimd", lambda e: e.dma_start(out=cos_sb[:, T:2 * T], in_=d_sin[:]),
           incs=[("gwc", 16)])
    w_cos = b.at("gwc")
    w_consts_gw = w_cos

    with nc.sbuf_tensor("fscr", [128, 3 * 1024], F32, side="right") as fscr, \
         nc.sbuf_tensor("h_sb", [128, EIT * T], BF16, side="right") as h_sb, \
         nc.sbuf_tensor("wdb1", [128, 4 * EIT * 128], BF16, side="right") as wdb1:
        w_y, _ = emit_ln(y_sb)
        w_router = emit_router(y_sb, w_y)
        ffn1_evac = emit_ffn(y_sb, fscr[:, 0:1024], fscr[:, 1024:2048],
                             fscr[:, 2048:3072], h_sb, wdb1, w_router)

    w_qrot = 0
    if stage >= 2:
        w_y2, y2_done = emit_ln(y_sb, cw=lambda c: [("ve", ffn1_evac[c])])
        ctxQ = ExitStack()
        q_sb = ctxQ.enter_context(nc.sbuf_tensor("q_sb", [128, H * T], BF16,
                                                 side="right"))
        mq2 = ctxQ.enter_context(nc.sbuf_tensor("mq2", [1, 64], F32, side="right"))
        with nc.sbuf_tensor("krot", [128, HKV * T], BF16, side="right") as krot, \
             nc.sbuf_tensor("vloc", [128, 8 * 512], BF16, side="right") as vloc, \
             nc.sbuf_tensor("wvb", [128, 2 * 512], BF16, side="right") as wvb:
            with ExitStack() as vs:
                vps = [vs.enter_context(nc.psum_tensor(_nm("p_v%d" % i), [128, 512], F32))
                       for i in range(8)]
                w_vmm = 0
                for c in range(DC):
                    gi_v, sl, wz, smv, tgv = sl_wv.acquire()
                    if wz == [] and c >= 2:
                        wz = [("mm", b.at("mm"))]
                    b.emit("gpsimd", (lambda e, c=c, sl=sl: e.dma_start(
                        out=wvb[:, sl * 512:(sl + 1) * 512],
                        in_=d_wv[c * 128:(c + 1) * 128, :])),
                        waits=wz, incs=[(smv, 16)])

                    def mm_v(e, c=c, sl=sl):
                        ins = None
                        for tc in range(8):
                            ins = e.matmul(
                                vps[tc][:],
                                y_sb[:, c * T + tc * 128: c * T + (tc + 1) * 128],
                                wvb[:, sl * 512:(sl + 1) * 512],
                                start=(c == 0), stop=(c == DC - 1))
                        return ins
                    b.emit("tensor", mm_v,
                           waits=[(smv, tgv), ("ve", y2_done[c])],
                           incs=[("mm", 1)])
                    sl_wv.consumed(gi_v, b.at("mm"))
                    w_vmm = b.at("mm")
                for tc in range(8):
                    b.emit("scalar", (lambda e, tc=tc: e.copy(
                        out=vloc[:, tc * 512:(tc + 1) * 512], in_=vps[tc][:])),
                        waits=[("mm", w_vmm)], incs=[("ac", 1)])
            w_vloc = b.at("ac")
            with nc.psum_tensor(_nm("p_k"), [128, 2 * T], F32) as p_k, \
                 nc.psum_tensor(_nm("p_kr"), [128, 2 * T], F32) as p_kr:
                pend = [None]
                copy_ac = {}
                for g in range(HKV):
                    sl = g % 2
                    gi_g, slg, wzg, smg, tgg = sl_wg.acquire()
                    b.emit("gpsimd", (lambda e, g=g, sl=slg: e.dma_start(
                        out=wgb[:, sl * 2048:(sl + 1) * 2048].rearrange(
                            "p (c i) -> p c i", c=DC),
                        in_=d_wk[:, g * 128:(g + 1) * 128].rearrange(
                            "(c p) i -> p c i", p=128))),
                        waits=wzg, incs=[(smg, 16)])
                    wpsum = [("ac", copy_ac[g - 2])] if g >= 2 else [("ac", w_vloc)]

                    def mm_k(e, sl=sl, slg=slg):
                        ins = None
                        for c in range(DC):
                            for hh in range(2):
                                ins = e.matmul(
                                    p_k[:, sl * T + hh * 512: sl * T + (hh + 1) * 512],
                                    wgb[:, slg * 2048 + c * 128: slg * 2048 + (c + 1) * 128],
                                    y_sb[:, c * T + hh * 512: c * T + (hh + 1) * 512],
                                    start=(c == 0), stop=(c == DC - 1))
                        return ins
                    b.emit("tensor", mm_k,
                           waits=[(smg, tgg), ("ve", w_y2), ("gwc", w_cos)] + wpsum,
                           incs=[("mm", 1)])
                    sl_wg.consumed(gi_g, b.at("mm"))
                    w_kmm = b.at("mm")
                    if pend[0] is not None:
                        pend[0]()

                    def mk_tail(g=g, sl=sl, w_kmm=w_kmm):
                        def _():
                            b.emit("scalar", (lambda e: e.copy(
                                out=krot[:, g * T:(g + 1) * T],
                                in_=p_k[:, sl * T:(sl + 1) * T])),
                                waits=[("mm", w_kmm)], incs=[("ac", 1)])
                            w_kn = b.at("ac")
                            copy_ac[g] = w_kn

                            def mm_kroll(e):
                                ins = None
                                for hh in range(2):
                                    ins = e.matmul(
                                        p_kr[:, sl * T + hh * 512: sl * T + (hh + 1) * 512],
                                        rollm_sb[:],
                                        krot[:, g * T + hh * 512: g * T + (hh + 1) * 512],
                                        start=True, stop=True)
                                return ins
                            wroll = [("ac", w_kn)]
                            wroll += [("ve", b.at("ve"))] if g >= 2 else                                      [("ac", w_vloc)]
                            b.emit("tensor", mm_kroll, waits=wroll, incs=[("mm", 1)])
                            w_rollmm = b.at("mm")
                            b.emit("vector", (lambda e: e.tensor_mul(
                                tscr[:, 0:T], krot[:, g * T:(g + 1) * T],
                                cos_sb[:, 0:T])),
                                waits=[("ac", w_kn), ("ve", b.at("ve"))],
                                incs=[("ve", 1)])
                            b.emit("vector", (lambda e: e.tensor_mul(
                                tscr[:, T:2 * T], p_kr[:, sl * T:(sl + 1) * T],
                                cos_sb[:, T:2 * T])),
                                waits=[("mm", w_rollmm), ("ve", b.at("ve"))],
                                incs=[("ve", 1)])
                            b.emit("vector", (lambda e: e.tensor_add(
                                krot[:, g * T:(g + 1) * T], tscr[:, 0:T],
                                tscr[:, T:2 * T])),
                                waits=[("ve", b.at("ve"))], incs=[("ve", 1)])
                        return _
                    pend[0] = mk_tail()
                pend[0]()
            w_krot = b.at("ve")
            with nc.psum_tensor(_nm("p_nl"), [1, 1024], F32) as p_nl:
                for g in range(HKV):
                    sl = g % 2
                    b.emit("scalar", (lambda e, g=g, sl=sl: e.activation(
                        out=sqb[:, sl * 1024:(sl + 1) * 1024],
                        in_=krot[:, g * T:(g + 1) * T], func=AF.Square)),
                        waits=[("ve", w_krot), ("mm", b.at("mm"))],
                        incs=[("ac", 1)])
                    w_ks = b.at("ac")

                    def mm_nl(e, sl=sl):
                        ins = None
                        for hh in range(2):
                            ins = e.matmul(
                                p_nl[0:1, hh * 512:(hh + 1) * 512], ones1[:],
                                sqb[:, sl * 1024 + hh * 512: sl * 1024 + (hh + 1) * 512],
                                start=True, stop=True)
                        return ins
                    b.emit("tensor", mm_nl,
                           waits=[("ac", w_ks), ("ve", b.at("ve"))],
                           incs=[("mm", 1)])
                    b.emit("vector", (lambda e, g=g: e.reduce_max(
                        mq2[0:1, 16 + g:17 + g], p_nl[0:1, 0:1024], axis=AX.X)),
                        waits=[("mm", b.at("mm"))], incs=[("ve", 1)])
            w_knloc = b.at("ve")
            b.emit("sync", lambda e: e.dma_start(
                out=d_kin.rearrange("(g p) t -> p g t", p=128),
                in_=krot[:].rearrange("p (g t) -> p g t", g=HKV)),
                waits=[("ve", w_krot)], incs=[("st", 16)])
            b.emit("sync", lambda e: e.dma_start(
                out=d_vin.rearrange("(c p) h -> p c h", p=128),
                in_=vloc[:].rearrange("p (c h) -> p c h", c=8)),
                waits=[("ac", w_vloc)], incs=[("st", 16)])
            b.emit("sync", lambda e: e.dma_start(
                out=d_nrmin[:], in_=mq2[0:1, 16:24]),
                waits=[("ve", w_knloc)], incs=[("st", 16)])
            w_kvst = b.at("st")
            b.emit("gpsimd", lambda e: e.collective_compute(
                "AllGather", ALU.bypass, replica_groups=[list(range(NCORES))],
                ins=[d_kin.opt()], outs=[d_kout.opt()]),
                waits=[("st", w_kvst)], incs=[("cc", 1)])
            b.emit("gpsimd", lambda e: e.collective_compute(
                "AllGather", ALU.bypass, replica_groups=[list(range(NCORES))],
                ins=[d_vin.opt()], outs=[d_vout.opt()]),
                incs=[("cc", 1)])
            b.emit("gpsimd", lambda e: e.collective_compute(
                "AllGather", ALU.bypass, replica_groups=[list(range(NCORES))],
                ins=[d_nrmin.opt()], outs=[d_nrmout.opt()]),
                incs=[("cc", 1)])
            w_cc = b.at("cc")
            w_cc_kv = w_cc - 1
            with nc.psum_tensor(_nm("p_q"), [128, 2 * T], F32) as p_q, \
                 nc.psum_tensor(_nm("p_qr"), [128, 2 * T], F32) as p_qr:
                pend = [None]
                copy_ac = {}
                for h in range(H):
                    sl = h % 2
                    gi_g, slg, wzg, smg, tgg = sl_wg.acquire()
                    b.emit("gpsimd", (lambda e, h=h, sl=slg: e.dma_start(
                        out=wgb[:, sl * 2048:(sl + 1) * 2048].rearrange(
                            "p (c i) -> p c i", c=DC),
                        in_=d_wq[:, h * 128:(h + 1) * 128].rearrange(
                            "(c p) i -> p c i", p=128))),
                        waits=wzg, incs=[(smg, 16)])
                    wpsum = [("ac", copy_ac[h - 2])] if h >= 2 else [("ve", w_krot)]

                    def mm_q(e, sl=sl, slg=slg):
                        ins = None
                        for c in range(DC):
                            for hh in range(2):
                                ins = e.matmul(
                                    p_q[:, sl * T + hh * 512: sl * T + (hh + 1) * 512],
                                    wgb[:, slg * 2048 + c * 128: slg * 2048 + (c + 1) * 128],
                                    y_sb[:, c * T + hh * 512: c * T + (hh + 1) * 512],
                                    start=(c == 0), stop=(c == DC - 1))
                        return ins
                    b.emit("tensor", mm_q, waits=[(smg, tgg)] + wpsum,
                           incs=[("mm", 1)])
                    sl_wg.consumed(gi_g, b.at("mm"))
                    w_qmm = b.at("mm")
                    if pend[0] is not None:
                        pend[0]()

                    def mk_tail(h=h, sl=sl, w_qmm=w_qmm):
                        def _():
                            b.emit("scalar", (lambda e: e.copy(
                                out=q_sb[:, h * T:(h + 1) * T],
                                in_=p_q[:, sl * T:(sl + 1) * T])),
                                waits=[("mm", w_qmm)], incs=[("ac", 1)])
                            w_qn = b.at("ac")
                            copy_ac[h] = w_qn

                            def mm_qroll(e):
                                ins = None
                                for hh in range(2):
                                    ins = e.matmul(
                                        p_qr[:, sl * T + hh * 512: sl * T + (hh + 1) * 512],
                                        rollm_sb[:],
                                        q_sb[:, h * T + hh * 512: h * T + (hh + 1) * 512],
                                        start=True, stop=True)
                                return ins
                            wroll = [("ac", w_qn)]
                            wroll += [("ve", b.at("ve"))] if h >= 2 else                                      [("ve", w_krot)]
                            b.emit("tensor", mm_qroll, waits=wroll, incs=[("mm", 1)])
                            w_rollmm = b.at("mm")
                            b.emit("vector", (lambda e: e.tensor_mul(
                                tscr[:, 0:T], q_sb[:, h * T:(h + 1) * T],
                                cos_sb[:, 0:T])),
                                waits=[("ac", w_qn), ("ve", b.at("ve"))],
                                incs=[("ve", 1)])
                            b.emit("vector", (lambda e: e.tensor_mul(
                                tscr[:, T:2 * T], p_qr[:, sl * T:(sl + 1) * T],
                                cos_sb[:, T:2 * T])),
                                waits=[("mm", w_rollmm), ("ve", b.at("ve"))],
                                incs=[("ve", 1)])
                            b.emit("vector", (lambda e: e.tensor_add(
                                q_sb[:, h * T:(h + 1) * T], tscr[:, 0:T],
                                tscr[:, T:2 * T])),
                                waits=[("ve", b.at("ve"))], incs=[("ve", 1)])
                        return _
                    pend[0] = mk_tail()
                pend[0]()
            w_qrot = b.at("ve")
            with nc.psum_tensor(_nm("p_nq"), [1, 1024], F32) as p_nq:
                for h in range(H):
                    sl = h % 2
                    wz = [("mm", b.at("mm"))] if h >= 2 else []
                    b.emit("scalar", (lambda e, h=h, sl=sl: e.activation(
                        out=sqb[:, sl * 1024:(sl + 1) * 1024],
                        in_=q_sb[:, h * T:(h + 1) * T], func=AF.Square)),
                        waits=[("ve", w_qrot)] + wz, incs=[("ac", 1)])
                    w_qs = b.at("ac")
                    wz2 = [("ve", b.at("ve"))] if h >= 1 else []

                    def mm_nq(e, sl=sl):
                        ins = None
                        for hh in range(2):
                            ins = e.matmul(
                                p_nq[:, hh * 512:(hh + 1) * 512], ones1[:],
                                sqb[:, sl * 1024 + hh * 512: sl * 1024 + (hh + 1) * 512],
                                start=True, stop=True)
                        return ins
                    b.emit("tensor", mm_nq, waits=[("ac", w_qs)] + wz2,
                           incs=[("mm", 1)])
                    b.emit("vector", (lambda e, h=h: e.reduce_max(
                        mq2[0:1, h:h + 1], p_nq[:], axis=AX.X)),
                        waits=[("mm", b.at("mm"))], incs=[("ve", 1)])
            w_mq2 = b.at("ve")
        ctxA.close()
        ctxS = ExitStack()
        kf_sb = ctxS.enter_context(nc.sbuf_tensor("kf", [128, NKC * 128], BF16))
        vf_sb = ctxS.enter_context(nc.sbuf_tensor("vf", [128, NKC * 128], BF16))
        oall = ctxS.enter_context(nc.sbuf_tensor("oall", [128, H * T], BF16))
        et_sb = xbb
        rb_sb = tscr[:, 0:1024]
        negb = tscr[:, 1024:1028]
        nk2 = tscr[0:1, 1056:1072]
        rs2 = tscr[0:2, 1280:1792]
        with nc.psum_tensor(_nm("p_s0"), [128, 1024], F32) as p_s0, \
             nc.psum_tensor(_nm("p_s1"), [128, 1024], F32) as p_s1, \
             nc.psum_tensor(_nm("p_o"), [128, 1024], F32) as p_o, \
             nc.psum_tensor(_nm("p_sum2"), [2, 512], F32) as p_sum2, \
             nc.psum_tensor(_nm("p_rb"), [128, 512], F32) as p_rb:
            ldrow = tscr[0:1, 1056:1120]
            negb_all = tscr[:, 1024:1040]
            b.emit("sync", lambda e: e.dma_start(
                out=ldrow.rearrange("o (r g) -> o r g", r=8),
                in_=d_nrmout.rearrange("(o r) g -> o r g", o=1)),
                waits=[("cc", w_cc), ("ve", w_qrot)], incs=[("kvn", 16)])
            for g in range(HKV):
                b.emit("vector", (lambda e, g=g: e.reduce_max(
                    mq2[0:1, 24 + g:25 + g],
                    ldrow.rearrange("o (r g) -> o r g", g=8)[:, :, g],
                    axis=AX.X)),
                    waits=[("kvn", 16), ("ve", b.at("ve"))], incs=[("ve", 1)])
                b.emit("vector", (lambda e, g=g: e.tensor_scalar(
                    mq2[0:1, 32 + 4 * g:36 + 4 * g],
                    mq2[0:1, 4 * g:4 * g + 4],
                    mq2[0:1, 24 + g:25 + g], 1.0 / 128.0, ALU.mult, ALU.mult)),
                    waits=[("ve", b.at("ve")), ("ve", w_mq2)], incs=[("ve", 1)])
            b.emit("scalar", lambda e: e.activation(
                out=mq2[0:1, 32:48], in_=mq2[0:1, 32:48], func=AF.Ln),
                waits=[("ve", b.at("ve")), ("ac", b.at("ac"))], incs=[("ac", 1)])
            b.emit("scalar", lambda e: e.activation(
                out=mq2[0:1, 32:48], in_=mq2[0:1, 32:48], func=AF.Exp, scale=0.5),
                waits=[("ac", b.at("ac"))], incs=[("ac", 1)])
            b.emit("vector", lambda e: e.tensor_scalar_mul(
                mq2[0:1, 32:48], mq2[0:1, 32:48], -1.0),
                waits=[("ac", b.at("ac")), ("ve", b.at("ve"))], incs=[("ve", 1)])
            w_bh = b.at("ve")
            b.emit("tensor", lambda e: e.matmul(
                p_o[:, 0:16], onesf[:], mq2[0:1, 32:48], start=True, stop=True),
                waits=[("ve", w_bh)], incs=[("mm", 1)])
            b.emit("scalar", lambda e: e.copy(out=negb_all, in_=p_o[:, 0:16]),
                   waits=[("mm", b.at("mm"))], incs=[("ac", 1)])
            w_negb = b.at("ac")
            last_s_mm = 0
            last_pv_mm = 0
            pending_epi = [None]
            prev_exp_last = [0]
            prev_evac = [b.at("ve")]
            for g in range(HKV):
                if pending_epi[0] is not None:
                    pending_epi[0]()
                    pending_epi[0] = None
                kld_waits = [("cc", w_cc_kv), ("mm", last_s_mm)]
                vld_waits = [("cc", w_cc_kv), ("mm", last_pv_mm)]
                if g == 0:
                    kld_waits.append(("ve", w_qrot))
                    vld_waits.append(("ve", w_qrot))
                b.emit("sync", (lambda e, g=g: e.dma_start(
                    out=kf_sb[:].rearrange("p (r t) -> p r t", r=NCORES),
                    in_=d_kout.rearrange("(r g p) t -> g p r t",
                                         r=NCORES, g=HKV)[g])),
                    waits=kld_waits, incs=[("kvk", 16)])
                w_kld = b.at("kvk")
                b.emit("sync", (lambda e, g=g: e.dma_start(
                    out=vf_sb[:].rearrange("p (kc hd) -> p kc hd", kc=NKC),
                    in_=d_vout.rearrange("(kc p) (g hd) -> g p kc hd",
                                         p=128, g=HKV)[g])),
                    waits=vld_waits, incs=[("kvv", 16)])
                w_vld = b.at("kvv")
                for hh4 in range(4):
                    h = g * 4 + hh4
                    exp_done = {}
                    pv_done = {}

                    def mm_s(e, kc, psl, h):
                        ins = None
                        for hh in range(2):
                            ins = e.matmul(
                                psl[:, hh * 512:(hh + 1) * 512],
                                kf_sb[:, kc * 128:(kc + 1) * 128],
                                q_sb[:, h * T + hh * 512: h * T + (hh + 1) * 512],
                                start=True, stop=True)
                        return ins

                    def mm_pv(e, kc, esl):
                        ins = None
                        for hh in range(2):
                            ins = e.matmul(
                                p_o[:, hh * 512:(hh + 1) * 512],
                                vf_sb[:, kc * 128:(kc + 1) * 128],
                                et_sb[:, esl * T + hh * 512: esl * T + (hh + 1) * 512],
                                start=(kc == 0), stop=(kc == NKC - 1))
                            ins = e.matmul(
                                p_sum2[:], sumsel[:, hh * 2:(hh + 1) * 2],
                                et_sb[:, esl * T + hh * 512: esl * T + (hh + 1) * 512],
                                start=(kc == 0 and hh == 0),
                                stop=(kc == NKC - 1 and hh == 1))
                        return ins

                    def emit_s(kc):
                        psl = p_s0 if kc % 2 == 0 else p_s1
                        wz = [("ac", exp_done[kc - 2])] if kc >= 2 else \
                             [("ac", prev_exp_last[0])]
                        b.emit("tensor",
                               (lambda e, kc=kc, psl=psl, h=h: mm_s(e, kc, psl, h)),
                               waits=[("kvk", w_kld), ("ve", w_qrot)] + wz,
                               incs=[("mm", 1)])
                        w_smm = b.at("mm")
                        wz2 = [("mm", pv_done[kc - 2])] if kc >= 2 else []
                        b.emit("scalar", (lambda e, psl=psl, esl=kc % 2, h=h:
                               e.activation(
                                   out=et_sb[:, esl * T:(esl + 1) * T], in_=psl[:],
                                   func=AF.Exp, bias=negb_all[:, h:h + 1],
                                   scale=ISQ)),
                               waits=[("mm", w_smm), ("ac", w_negb)] + wz2,
                               incs=[("ac", 1)])
                        exp_done[kc] = b.at("ac")
                        return w_smm

                    def emit_pv(kc):
                        wz = [("ve", prev_evac[0])] if kc == 0 else []
                        b.emit("tensor", (lambda e, kc=kc, esl=kc % 2: mm_pv(e, kc, esl)),
                               waits=[("ac", exp_done[kc]), ("kvv", w_vld)] + wz,
                               incs=[("mm", 1)])
                        pv_done[kc] = b.at("mm")
                        return b.at("mm")

                    last_s_mm = emit_s(0)
                    last_s_mm = emit_s(1)
                    if pending_epi[0] is not None:
                        pending_epi[0]()
                        pending_epi[0] = None
                    last_pv_mm = emit_pv(0)
                    for kc in range(2, NKC):
                        last_s_mm = emit_s(kc)
                        last_pv_mm = emit_pv(kc - 1)
                    last_pv_mm = emit_pv(NKC - 1)

                    def make_epi(h=h, w_hd=last_pv_mm):
                        def _():
                            b.emit("vector", lambda e: e.reciprocal(rs2, p_sum2[:]),
                                   waits=[("mm", w_hd), ("ve", b.at("ve"))],
                                   incs=[("ve", 1)])
                            w_rs = b.at("ve")
                            for hh in range(2):
                                b.emit("tensor", (lambda e, hh=hh: e.matmul(
                                    p_rb[:], rowsel2[:, hh * 128:(hh + 1) * 128],
                                    rs2, start=True, stop=True)),
                                    waits=[("ve", w_rs), ("ac", b.at("ac"))],
                                    incs=[("mm", 1)])
                                b.emit("scalar", (lambda e, hh=hh: e.copy(
                                    out=rb_sb[:, hh * 512:(hh + 1) * 512],
                                    in_=p_rb[:])),
                                    waits=[("mm", b.at("mm"))], incs=[("ac", 1)])
                                w_cp = b.at("ac")
                                b.emit("vector", (lambda e, h=h, hh=hh: e.tensor_mul(
                                    oall[:, h * T + hh * 512: h * T + (hh + 1) * 512],
                                    p_o[:, hh * 512:(hh + 1) * 512],
                                    rb_sb[:, hh * 512:(hh + 1) * 512])),
                                    waits=[("ac", w_cp)], incs=[("ve", 1)])
                            prev_evac[0] = b.at("ve")
                        return _
                    pending_epi[0] = make_epi()
                    prev_exp_last[0] = exp_done[NKC - 1]
            if pending_epi[0] is not None:
                pending_epi[0]()
                pending_epi[0] = None
        w_sdpa = b.at("ve")
        with nc.psum_tensor(_nm("p_w0"), [128, 1024], F32) as p_w0, \
             nc.psum_tensor(_nm("p_w1"), [128, 1024], F32) as p_w1:
            wo_evac = {}
            for dt in range(DC):
                pw = p_w0 if dt % 2 == 0 else p_w1
                gi_g, slg, wzg, smg, tgg = sl_wg.acquire()
                b.emit("gpsimd", (lambda e, dt=dt, sl=slg: e.dma_start(
                    out=wgb[:, sl * 2048:(sl + 1) * 2048].rearrange(
                        "p (c i) -> p c i", c=DC),
                    in_=d_wo[:, dt * 128:(dt + 1) * 128].rearrange(
                        "(c p) i -> p c i", p=128))),
                    waits=wzg, incs=[(smg, 16)])
                wz2 = [("ve", wo_evac[dt - 2])] if dt >= 2 else []

                def mm_wo(e, sl=slg, pw=pw):
                    ins = None
                    for hc in range(H):
                        for hh in range(2):
                            ins = e.matmul(
                                pw[:, hh * 512:(hh + 1) * 512],
                                wgb[:, sl * 2048 + hc * 128: sl * 2048 + (hc + 1) * 128],
                                oall[:, hc * T + hh * 512: hc * T + (hh + 1) * 512],
                                start=(hc == 0), stop=(hc == H - 1))
                    return ins
                b.emit("tensor", mm_wo,
                       waits=[(smg, tgg), ("ve", w_sdpa)] + wz2, incs=[("mm", 1)])
                sl_wg.consumed(gi_g, b.at("mm"))
                w_womm = b.at("mm")
                b.emit("vector", (lambda e, dt=dt, pw=pw: e.tensor_add(
                    x_sb[:, dt * T:(dt + 1) * T], x_sb[:, dt * T:(dt + 1) * T],
                    pw[:])),
                    waits=[("mm", w_womm)], incs=[("ve", 1)])
                wo_evac[dt] = b.at("ve")
        attn_evac = dict(wo_evac)
        ctxS.close()
        ctxQ.close()
    else:
        ctxA.close()

    if stage >= 3:
        ctxF = ExitStack()
        y2_sb = ctxF.enter_context(nc.sbuf_tensor("y2_sb", [128, DC * T], BF16))
        fscr2 = ctxF.enter_context(nc.sbuf_tensor("fscr2", [128, 3 * 1024], F32,
                                                  side="right"))
        h_sb2 = ctxF.enter_context(nc.sbuf_tensor("h_sb2", [128, EIT * T], BF16,
                                                  side="right"))
        wdb2 = ctxF.enter_context(nc.sbuf_tensor("wdb2", [128, 4 * EIT * 128],
                                                 BF16, side="right"))
        w_y3, _ = emit_ln(y2_sb, cw=lambda c: [("ve", attn_evac[c])])
        w_router3 = emit_router(y2_sb, w_y3)
        emit_ffn(y2_sb, fscr2[:, 0:1024], fscr2[:, 1024:2048],
                 fscr2[:, 2048:3072], h_sb2, wdb2, w_router3)
        ctxF.close()

    b.emit("sync", lambda e: e.dma_start(
        out=d_out.rearrange("(k p) t -> p k t", p=128),
        in_=x_sb[:].rearrange("p (k t) -> p k t", k=DC)),
        waits=[("ve", b.at("ve")), ("mm", b.at("mm")), ("ac", b.at("ac"))],
        incs=[("st", 16)])
    w_out = b.at("st")
    b.emit("sync", (lambda e, w=w_out: e.wait_ge(sems["st"], w)))

    with nc.Block() as block:
        def mk_runner(eng_ops):
            def run(eng):
                for waits, fn, incs in eng_ops:
                    for s, v in waits:
                        eng.wait_ge(sems[s], v)
                    ins = fn(eng)
                    if incs:
                        assert ins is not None, "then_inc needs an instruction"
                        for s, a in incs:
                            ins = ins.then_inc(sems[s], a)
            return run
        block.sync(mk_runner(b.ops["sync"]))
        block.gpsimd(mk_runner(b.ops["gpsimd"]))
        block.tensor(mk_runner(b.ops["tensor"]))
        block.vector(mk_runner(b.ops["vector"]))
        block.scalar(mk_runner(b.ops["scalar"]))

    es.close()
    return nc


def kernel(L0_attn_x, L0_attn_Wq, L0_attn_Wk, L0_attn_Wv, L0_attn_Wo,
           L0_ffn_router_w, L0_ffn_w_gate, L0_ffn_w_up, L0_ffn_w_down,
           L0_ffn_sw_gate, L0_ffn_sw_up, L0_ffn_sw_down, L0_ffn_shared_gate_w):
    x = np.asarray(L0_attn_x)
    assert x.shape == (1, 8192, 2048)
    xt = np.ascontiguousarray(x[0].T.astype(np.float32))

    wq = np.ascontiguousarray(np.asarray(L0_attn_Wq, dtype=np.float32))
    wk = np.ascontiguousarray(np.asarray(L0_attn_Wk, dtype=np.float32))
    wv = np.ascontiguousarray(np.asarray(L0_attn_Wv, dtype=np.float32))
    wo = np.ascontiguousarray(np.asarray(L0_attn_Wo, dtype=np.float32))
    rwT = np.ascontiguousarray(np.asarray(L0_ffn_router_w, dtype=np.float32).T)
    sgw = np.ascontiguousarray(
        np.asarray(L0_ffn_shared_gate_w, dtype=np.float32).reshape(16, 128).T)
    wg = np.ascontiguousarray(np.asarray(L0_ffn_w_gate, dtype=np.float32))
    wu = np.ascontiguousarray(np.asarray(L0_ffn_w_up, dtype=np.float32))
    wd = np.ascontiguousarray(np.asarray(L0_ffn_w_down, dtype=np.float32))
    swg = np.ascontiguousarray(np.asarray(L0_ffn_sw_gate, dtype=np.float32))
    swu = np.ascontiguousarray(np.asarray(L0_ffn_sw_up, dtype=np.float32))
    swd = np.ascontiguousarray(np.asarray(L0_ffn_sw_down, dtype=np.float32))

    inv = 1.0 / (10000.0 ** (np.arange(0, 128, 2, dtype=np.float64) / 128.0))
    pos = np.arange(8192, dtype=np.float64)
    freqs = np.concatenate([inv, inv])
    emb = freqs[:, None] * pos[None, :]
    cos_t = np.cos(emb).astype(np.float32)
    sin_t = np.sin(emb).astype(np.float32)

    ones1 = np.ones((128, 1), np.float32)
    onesf = np.ones((1, 128), np.float32)
    halff = np.full((1, 128), 0.5, np.float32)
    identb = np.eye(128, dtype=np.float32)
    rollm = np.zeros((128, 128), np.float32)
    rollm[(np.arange(128) - 1) % 128, np.arange(128)] = 1.0
    sumsel = np.zeros((128, 4), np.float32)
    sumsel[:, 0] = 1.0; sumsel[:, 3] = 1.0
    rowsel2 = np.zeros((2, 256), np.float32)
    rowsel2[0, 0:128] = 1.0; rowsel2[1, 128:256] = 1.0
    sel05 = np.zeros((8, 8 * 128), np.float32)
    for e in range(8):
        sel05[e, e * 128:(e + 1) * 128] = 0.5

    nc = build_program(STAGE)
    in_maps = []
    for c in range(NCORES):
        sl = slice(c * T, (c + 1) * T)
        in_maps.append({
            "x": np.ascontiguousarray(xt[:, sl]),
            "cos_t": np.ascontiguousarray(cos_t[:, sl]),
            "sin_t": np.ascontiguousarray(sin_t[:, sl]),
            "wq": wq, "wk": wk, "wv": wv, "wo": wo, "rollm": rollm,
            "rwT": rwT, "sgw": sgw,
            "wg": wg, "wu": wu, "wd": wd,
            "swg": swg, "swu": swu, "swd": swd,
            "ones1": ones1, "onesf": onesf, "halff": halff,
            "identb": identb, "sel05": sel05, "sumsel": sumsel,
            "rowsel2": rowsel2,
        })
    res = run_bass_kernel_spmd(nc, in_maps, core_ids=list(range(NCORES)))
    out = np.concatenate([r["out"] for r in res.results], axis=1)
    return np.ascontiguousarray(out.T).reshape(1, 8192, 2048).astype(np.float32)
